# version 21
# baseline (speedup 1.0000x reference)
import numpy as np
from contextlib import ExitStack

import concourse.bass as bass
import concourse.mybir as mybir
from concourse.bass_utils import run_bass_kernel_spmd

F32 = mybir.dt.float32
BF16 = mybir.dt.bfloat16
AF = mybir.ActivationFunctionType
ALU = mybir.AluOpType
AX = mybir.AxisListType

NG = 16
NPAIR = 64
NOG = 64
MAGIC = 12582912.0
LN127 = float(np.log(127.0))
FULL = 99

ENGINES = ("sync", "tensor", "vector", "scalar", "gpsimd")
OWN = {"vector": "DV", "scalar": "AC", "gpsimd": "PL"}
DMA_SEMS = (["LC", "LV", "LB", "TB", "TZ"]
            + [f"LW{i}" for i in range(4)] + ["LG0", "LG1"]
            + [f"LT{i}" for i in range(4)] + ["LR0", "LR1"]
            + [f"LS{i}" for i in range(10)]
            + [f"TT{i}" for i in range(4)] + ["TG0", "TG1", "TR0", "TR1"]
            + [f"TS{i}" for i in range(8)] + [f"TA{i}" for i in range(8)])


class Prog:
    def __init__(self):
        self.q = {e: [] for e in ENGINES}
        self.cnt = {}
        self.last_wait = {e: {} for e in ENGINES}
        self.sems = {}

    def new_sem(self, name):
        self.cnt[name] = 0

    def wait(self, eng, sem, val=None):
        if val is None:
            val = self.cnt[sem]
        if val <= 0 or self.last_wait[eng].get(sem, 0) >= val:
            return
        self.last_wait[eng][sem] = val
        self.q[eng].append(lambda e, s=sem, v=val: e.wait_ge(self.sems[s], v))

    def op(self, eng, fn, inc=None, n=1, dma=False):
        if not dma and eng in OWN:
            self.wait(eng, OWN[eng])
            if inc is None:
                inc = OWN[eng]
        if inc is None:
            self.q[eng].append(fn)
            return None
        self.cnt[inc] += n
        tgt = self.cnt[inc]

        def run(e, f=fn, s=inc, k=n):
            f(e).then_inc(self.sems[s], k)
        self.q[eng].append(run)
        return tgt

    def dma(self, eng, sem, fn):
        return self.op(eng, fn, inc=sem, n=16, dma=True)

    def barrier(self):
        snap = dict(self.cnt)
        for eng in ENGINES:
            for s, v in snap.items():
                self.wait(eng, s, v)


def build(stage=FULL, dbg=False):
    nc = bass.Bass("TRN2", target_bir_lowering=False)
    kscr = "ExternalOutput" if dbg else "Internal"

    x_cT = nc.dram_tensor("x_cT", [2048, 2048], F32, kind="ExternalInput").ap()
    wgu = nc.dram_tensor("wgu", [128, 128, 2048], F32, kind="ExternalInput").ap()
    wdn = nc.dram_tensor("wdn", [64, 128, 2048], F32, kind="ExternalInput").ap()
    wk = nc.dram_tensor("wk", [16, 128, 2048], F32, kind="ExternalInput").ap()
    wv = nc.dram_tensor("wv", [16, 128, 2048], F32, kind="ExternalInput").ap()
    wr = nc.dram_tensor("wr", [16, 128, 2048], F32, kind="ExternalInput").ap()
    wo = nc.dram_tensor("wo", [16, 128, 2048], F32, kind="ExternalInput").ap()
    tvecs = nc.dram_tensor("tvecs", [5, 16, 128], F32, kind="ExternalInput").ap()
    eyef_in = nc.dram_tensor("eyef", [128, 128], F32, kind="ExternalInput").ap()
    consts_in = nc.dram_tensor("consts", [128, 128], F32, kind="ExternalInput").ap()
    constsb_in = nc.dram_tensor("constsb", [128, 128], BF16, kind="ExternalInput").ap()
    onesrow_in = nc.dram_tensor("onesrow", [1, 128], F32, kind="ExternalInput").ap()

    out_cT = nc.dram_tensor("out_cT", [16, 128, 2048], F32, kind="ExternalOutput").ap()

    x1d = nc.dram_tensor("x1d", [16, 128, 2048], F32, kind=kscr).ap()
    x2d = nc.dram_tensor("x2d", [16, 128, 2048], F32, kind=kscr).ap()
    g1d = nc.dram_tensor("g1d", [4, 128, 32768], F32, kind=kscr).ap()
    zd = nc.dram_tensor("zd", [16, 128, 2048], BF16, kind=kscr).ap()
    terngu = nc.dram_tensor("terngu", [128, 128, 2048], BF16, kind="Internal").ap()
    terndn = nc.dram_tensor("terndn", [4, 128, 32768], BF16, kind="Internal").ap()
    vsp = nc.dram_tensor("vsp", [16, 128, 2048], F32, kind="Internal").ap()
    kvsp = nc.dram_tensor("kvsp", [16, 128, 2048], F32, kind=kscr).ap()
    srsp = nc.dram_tensor("srsp", [16, 128, 2048], F32, kind=kscr).ap()
    bounce = nc.dram_tensor("bounce", [4, 2048], F32, kind="Internal").ap()

    p = Prog()
    for s in ("PE", "DV", "AC", "PL"):
        p.new_sem(s)
    for s in DMA_SEMS:
        p.new_sem(s)

    with ExitStack() as stack:
        en = stack.enter_context
        for s in list(p.cnt):
            p.sems[s] = en(nc.semaphore(s))

        bfA = en(nc.sbuf_tensor("bfA", [128, 32768], BF16))
        wbuf = en(nc.sbuf_tensor("wbuf", [128, 8192], F32))
        gsb = en(nc.sbuf_tensor("gsb", [128, 4096], F32))
        tern = en(nc.sbuf_tensor("tern", [128, 8192], BF16))
        bcst = en(nc.sbuf_tensor("bcst", [128, 4096], F32))
        rows = en(nc.sbuf_tensor("rows", [128, 4096], F32))
        rows2 = en(nc.sbuf_tensor("rows2", [128, 4096], F32))
        macc = en(nc.sbuf_tensor("macc", [128, 1024], F32))
        tmp = en(nc.sbuf_tensor("tmp", [128, 2048], F32))
        cols = en(nc.sbuf_tensor("cols", [128, 128], F32))
        eyef = en(nc.sbuf_tensor("eyef_s", [128, 128], F32))
        consts = en(nc.sbuf_tensor("consts_s", [128, 128], F32))
        constsb = en(nc.sbuf_tensor("constsb_s", [128, 128], BF16))
        vstage = en(nc.sbuf_tensor("vstage", [16, 128], F32))
        onesr = en(nc.sbuf_tensor("onesr", [1, 128], F32))
        pb = [en(nc.psum_tensor(f"pb{i}", [128, 512], F32)) for i in range(8)]
        nc.const_aps.register(F32, 1e-5, consts[:, 1:2])
        nc.const_aps.register(F32, LN127, consts[:, 2:3])

        WS = lambda i: wbuf[:, i * 2048:(i + 1) * 2048]
        GS = lambda i: gsb[:, i * 2048:(i + 1) * 2048]
        TT = lambda i: tern[:, i * 2048:(i + 1) * 2048]
        BC = lambda i: bcst[:, i * 2048:(i + 1) * 2048]
        AR = lambda g: bfA[:, g * 2048:(g + 1) * 2048]
        AR512 = lambda i: bfA[:, i * 512:(i + 1) * 512]
        RS = lambda i: gsb[:, 1024 + i * 2048:1536 + i * 2048]

        def ROW(i):
            i = i % 4
            t_ = rows if i < 2 else rows2
            c0 = (i % 2) * 2048
            return t_[0:1, c0:c0 + 2048]

        R_M, R_RSTD, R_CS, R_PHI, R_Q2, R_F2 = 0, 1, 2, 3, 4, 5
        COL = lambda i: cols[:, i:i + 1]
        CEW, CU, CTMK, CTMV, CTMR = 0, 16, 32, 48, 64
        CSH_GU, CNSH_GU, CSH_DN, CNSH_DN = 80, 81, 82, 83
        CTMP0, CTMP1 = 84, 85
        CCAR = 88
        CMAXD = 96
        onesrow_f = onesr[0:1, :]
        onescol_f = consts[:, 0:1]
        onescol_b = constsb[:, 0:1]

        bank_free = [(None, 0)] * 8

        def pe_use_bank(b):
            sem, v = bank_free[b]
            if sem is not None:
                p.wait("tensor", sem, v)

        def set_bank_free(b, sem="DV"):
            bank_free[b] = (sem, p.cnt[sem])

        def bcast_row(row_ap, dst, banks=(6, 7)):
            for i in range(4):
                b = banks[i % 2]
                pe_use_bank(b)
                p.wait("tensor", "DV", p.cnt["DV"])
                p.op("tensor", lambda e, i=i, b=b: e.matmul(
                    out=pb[b][:], lhsT=onesrow_f, rhs=row_ap[:, i * 512:(i + 1) * 512],
                    start=True, stop=True), inc="PE")
                p.wait("vector", "PE", p.cnt["PE"])
                p.op("vector", lambda e, i=i, b=b: e.tensor_copy(
                    out=dst[:, i * 512:(i + 1) * 512], in_=pb[b][:]))
                set_bank_free(b)

        def row_from_cols(col16, dst_row, bslot, bank=6):
            pe_use_bank(bank)
            p.wait("tensor", "DV", p.cnt["DV"])
            p.op("tensor", lambda e: e.transpose(out=pb[bank][0:16, 0:128], in_=col16,
                                                 identity=eyef[:]), inc="PE")
            p.wait("vector", "PE", p.cnt["PE"])
            p.wait("vector", "TB", p.cnt["TB"])
            p.op("vector", lambda e: e.tensor_copy(out=vstage[:],
                                                   in_=pb[bank][0:16, 0:128]))
            set_bank_free(bank)
            p.wait("gpsimd", "DV", p.cnt["DV"])
            p.dma("gpsimd", "TB", lambda e: e.dma_start(
                out=bounce[bslot:bslot + 1, :].rearrange("o (a b) -> (o a) b", a=16),
                in_=vstage[:]))
            p.wait("sync", "TB", p.cnt["TB"])
            p.wait("sync", "DV", p.cnt["DV"])
            p.dma("sync", "LB", lambda e: e.dma_start(out=dst_row,
                                                      in_=bounce[bslot:bslot + 1, :]))
            for eng in ("vector", "scalar", "tensor"):
                p.wait(eng, "LB", p.cnt["LB"])

        def scalar_final(sum_col_ap, csh, cnsh, nelem, banks=(6, 7)):
            b0, b1 = banks
            pe_use_bank(b0)
            p.wait("tensor", "DV", p.cnt["DV"])
            p.op("tensor", lambda e, b0=b0: e.matmul(
                out=pb[b0][0:1, 0:1], lhsT=onescol_f,
                rhs=sum_col_ap, start=True, stop=True), inc="PE")
            p.wait("vector", "PE", p.cnt["PE"])
            p.op("vector", lambda e, b0=b0: e.tensor_scalar(
                out=cols[0:1, CTMP0:CTMP0 + 1], in0=pb[b0][0:1, 0:1],
                scalar1=1.0 / nelem, scalar2=1e-5, op0=ALU.mult, op1=ALU.max))
            set_bank_free(b0)
            pe_use_bank(b1)
            p.wait("tensor", "DV", p.cnt["DV"])
            p.op("tensor", lambda e, b1=b1: e.matmul(
                out=pb[b1][:, 0:1], lhsT=onesrow_f,
                rhs=cols[0:1, CTMP0:CTMP0 + 1],
                start=True, stop=True), inc="PE")
            p.wait("vector", "PE", p.cnt["PE"])
            p.op("vector", lambda e, csh=csh, b1=b1: e.tensor_scalar(
                out=COL(csh), in0=pb[b1][:, 0:1], scalar1=0.5, scalar2=None,
                op0=ALU.mult))
            p.op("vector", lambda e, cnsh=cnsh, b1=b1: e.tensor_scalar(
                out=COL(cnsh), in0=pb[b1][:, 0:1], scalar1=-0.5, scalar2=None,
                op0=ALU.mult))
            set_bank_free(b1)

        p.dma("sync", "LC", lambda e: e.dma_start(out=eyef[:], in_=eyef_in))
        p.dma("sync", "LC", lambda e: e.dma_start(out=consts[:], in_=consts_in))
        p.dma("sync", "LC", lambda e: e.dma_start(out=constsb[:], in_=constsb_in))
        p.dma("sync", "LC", lambda e: e.dma_start(out=onesr[:], in_=onesrow_in))
        for eng in ("tensor", "vector", "scalar", "gpsimd"):
            p.wait(eng, "LC", p.cnt["LC"])

        for vi, cb in ((0, CEW), (1, CU), (2, CTMK), (3, CTMV), (4, CTMR)):
            p.dma("sync", "LV", lambda e, vi=vi: e.dma_start(out=vstage[:],
                                                             in_=tvecs[vi]))
            p.wait("tensor", "LV", p.cnt["LV"])
            pe_use_bank(6)
            p.op("tensor", lambda e: e.transpose(out=pb[6][0:128, 0:16], in_=vstage[:],
                                                 identity=eyef[0:16, 0:16]), inc="PE")
            p.wait("vector", "PE", p.cnt["PE"])
            p.op("vector", lambda e, cb=cb: e.tensor_copy(out=cols[:, cb:cb + 16],
                                                          in_=pb[6][0:128, 0:16]))
            set_bank_free(6)
            p.wait("sync", "PE", p.cnt["PE"])
        p.wait("scalar", "DV", p.cnt["DV"])
        p.op("scalar", lambda e: e.activation(out=cols[:, CEW:CEW + 16],
                                              in_=cols[:, CEW:CEW + 16], func=AF.Exp))
        p.op("scalar", lambda e: e.activation(out=cols[:, CEW:CEW + 16],
                                              in_=cols[:, CEW:CEW + 16], func=AF.Exp,
                                              scale=-1.0))
        p.op("scalar", lambda e: e.activation(out=cols[:, 112:128],
                                              in_=cols[:, CU:CU + 16], func=AF.Exp))
        p.wait("vector", "AC", p.cnt["AC"])
        p.wait("gpsimd", "AC", p.cnt["AC"])

        x_chunk = lambda g: x_cT[g * 128:(g + 1) * 128, :]
        wpart = {}
        ln_cdv, ln_cac = {}, {}
        for b in (4, 5, 6, 7):
            pe_use_bank(b)

        def ln1_chunk(g):
            sl = g % 2
            sem = f"LG{sl}"
            if g >= 2:
                p.wait("sync", "DV", ln_cdv[g - 2])
                p.wait("sync", "AC", ln_cac[g - 2])
            p.dma("sync", sem, lambda e, g=g, sl=sl: e.dma_start(out=GS(sl),
                                                                in_=x_chunk(g)))
            ldc = p.cnt[sem]
            p.wait("vector", sem, ldc)
            p.wait("vector", "PE", p.cnt["PE"] - 8 if g >= 1 else 0)
            p.op("vector", lambda e, sl=sl: e.tensor_copy(out=TT(sl), in_=GS(sl)))
            ln_cdv[g] = p.cnt["DV"]
            p.wait("scalar", sem, ldc)
            p.wait("scalar", "PE", p.cnt["PE"] - 8 if g >= 1 else 0)
            p.op("scalar", lambda e, sl=sl: e.activation(out=TT(2 + sl), in_=GS(sl),
                                                         func=AF.Square))
            ln_cac[g] = p.cnt["AC"]
            p.wait("tensor", "DV", ln_cdv[g])
            p.wait("tensor", "AC", ln_cac[g])
            for t in range(4):
                p.op("tensor", lambda e, sl=sl, t=t, g=g: e.matmul(
                    out=pb[4 + t][0:1, :], lhsT=onescol_b,
                    rhs=TT(sl)[:, t * 512:(t + 1) * 512],
                    start=(g == 0), stop=(g == NG - 1)), inc="PE")
            for t in range(4):
                p.op("tensor", lambda e, sl=sl, t=t, g=g: e.matmul(
                    out=pb[4 + t][32:33, :], lhsT=onescol_b,
                    rhs=TT(2 + sl)[:, t * 512:(t + 1) * 512],
                    start=(g == 0), stop=(g == NG - 1)), inc="PE")

        for j in range(64):
            sl = j % 2
            eng0 = "sync" if sl == 0 else "gpsimd"
            sem = "LW0" if sl == 0 else "LS8"
            if j >= 2:
                p.wait(eng0, "AC", wpart[j - 2])
            p.dma(eng0, sem, lambda e, j=j, sl=sl: e.dma_start(
                out=wbuf[:, sl * 4096:(sl + 1) * 4096].rearrange(
                    "p (s f) -> p s f", f=2048),
                in_=wgu[2 * j:2 * j + 2].rearrange("o p h -> p o h")))
            p.wait("scalar", sem, p.cnt[sem])
            p.op("scalar", lambda e, j=j, sl=sl: e.activation(
                out=wbuf[:, sl * 4096:(sl + 1) * 4096],
                in_=wbuf[:, sl * 4096:(sl + 1) * 4096], func=AF.Abs,
                accum_out=macc[:, j:j + 1]))
            wpart[j] = p.cnt["AC"]
            if j % 4 == 0:
                ln1_chunk(j // 4)
        p.wait("vector", "AC", p.cnt["AC"])
        p.op("vector", lambda e: e.tensor_reduce(out=COL(CTMP1), in_=macc[:, 0:64],
                                                 axis=AX.X, op=ALU.add))
        scalar_final(COL(CTMP1), CSH_GU, CNSH_GU, 2048.0 * 16384.0, banks=(0, 1))
        p.wait("vector", "PE", p.cnt["PE"])
        m_row, rstd_row = ROW(R_M), ROW(R_RSTD)
        for t in range(4):
            p.op("vector", lambda e, t=t: e.tensor_scalar(
                out=m_row[:, t * 512:(t + 1) * 512], in0=pb[4 + t][0:1, :],
                scalar1=1.0 / 2048.0, scalar2=None, op0=ALU.mult))
        for t in range(4):
            p.op("vector", lambda e, t=t: e.tensor_scalar(
                out=rstd_row[:, t * 512:(t + 1) * 512], in0=pb[4 + t][32:33, :],
                scalar1=1.0 / 2048.0, scalar2=None, op0=ALU.mult))
        for b in (4, 5, 6, 7):
            set_bank_free(b)
        p.op("vector", lambda e: e.tensor_tensor(out=tmp[0:1, :], in0=m_row,
                                                 in1=m_row, op=ALU.mult))
        p.op("vector", lambda e: e.tensor_tensor(out=rstd_row, in0=rstd_row,
                                                 in1=tmp[0:1, :], op=ALU.subtract))
        p.wait("scalar", "DV", p.cnt["DV"])
        p.op("scalar", lambda e: e.activation(out=rstd_row, in_=rstd_row,
                                              func=AF.Ln, bias=1e-5))
        p.op("scalar", lambda e: e.activation(out=rstd_row, in_=rstd_row,
                                              func=AF.Exp, scale=-0.5))
        p.wait("vector", "AC", p.cnt["AC"])
        p.wait("tensor", "AC", p.cnt["AC"])

        def ln_stats(src, m_row, rstd_row):
            for b in (4, 5, 6, 7):
                pe_use_bank(b)
            cdv, cac = {}, {}
            for g in range(NG):
                sl = g % 2
                sem = f"LG{sl}"
                if g >= 2:
                    p.wait("sync", "DV", cdv[g - 2])
                    p.wait("sync", "AC", cac[g - 2])
                p.dma("sync", sem, lambda e, g=g, sl=sl: e.dma_start(out=GS(sl),
                                                                     in_=src(g)))
                ldc = p.cnt[sem]
                p.wait("vector", sem, ldc)
                p.wait("vector", "PE", p.cnt["PE"] - 8 if g >= 1 else 0)
                p.op("vector", lambda e, sl=sl: e.tensor_copy(out=TT(sl), in_=GS(sl)))
                cdv[g] = p.cnt["DV"]
                p.wait("scalar", sem, ldc)
                p.wait("scalar", "PE", p.cnt["PE"] - 8 if g >= 1 else 0)
                p.op("scalar", lambda e, sl=sl: e.activation(out=TT(2 + sl), in_=GS(sl),
                                                             func=AF.Square))
                cac[g] = p.cnt["AC"]
                p.wait("tensor", "DV", cdv[g])
                p.wait("tensor", "AC", cac[g])
                for t in range(4):
                    p.op("tensor", lambda e, sl=sl, t=t, g=g: e.matmul(
                        out=pb[4 + t][0:1, :], lhsT=onescol_b,
                        rhs=TT(sl)[:, t * 512:(t + 1) * 512],
                        start=(g == 0), stop=(g == NG - 1)), inc="PE")
                for t in range(4):
                    p.op("tensor", lambda e, sl=sl, t=t, g=g: e.matmul(
                        out=pb[4 + t][32:33, :], lhsT=onescol_b,
                        rhs=TT(2 + sl)[:, t * 512:(t + 1) * 512],
                        start=(g == 0), stop=(g == NG - 1)), inc="PE")
            p.wait("vector", "PE", p.cnt["PE"])
            for t in range(4):
                p.op("vector", lambda e, t=t: e.tensor_scalar(
                    out=m_row[:, t * 512:(t + 1) * 512], in0=pb[4 + t][0:1, :],
                    scalar1=1.0 / 2048.0, scalar2=None, op0=ALU.mult))
            for t in range(4):
                p.op("vector", lambda e, t=t: e.tensor_scalar(
                    out=rstd_row[:, t * 512:(t + 1) * 512], in0=pb[4 + t][32:33, :],
                    scalar1=1.0 / 2048.0, scalar2=None, op0=ALU.mult))
            for b in (4, 5, 6, 7):
                set_bank_free(b)
            p.op("vector", lambda e: e.tensor_tensor(out=tmp[0:1, :], in0=m_row,
                                                     in1=m_row, op=ALU.mult))
            p.op("vector", lambda e: e.tensor_tensor(out=rstd_row, in0=rstd_row,
                                                     in1=tmp[0:1, :], op=ALU.subtract))
            p.wait("scalar", "DV", p.cnt["DV"])
            p.op("scalar", lambda e: e.activation(out=rstd_row, in_=rstd_row,
                                                  func=AF.Ln, bias=1e-5))
            p.op("scalar", lambda e: e.activation(out=rstd_row, in_=rstd_row,
                                                  func=AF.Exp, scale=-0.5))
            p.wait("vector", "AC", p.cnt["AC"])
            p.wait("tensor", "AC", p.cnt["AC"])

        p.barrier()
        if stage <= 0:
            return _finish(p, nc)

        def ln_maxd(src, m_row, do_wdn=False):
            bcast_row(m_row, BC(0))
            sub_dv = {}
            wdn_dv = {}
            for g in range(NG):
                sl = g % 2
                sem = f"LG{sl}"
                if g >= 2:
                    p.wait("sync", "DV", sub_dv[g - 2])
                p.dma("sync", sem, lambda e, g=g, sl=sl: e.dma_start(out=GS(sl),
                                                                     in_=src(g)))
                if do_wdn:
                    for q4 in range(4):
                        og = g * 4 + q4
                        wsl = og % 2
                        eng0 = "sync" if wsl == 0 else "gpsimd"
                        wsem = "LW0" if wsl == 0 else "LS9"
                        if og >= 2:
                            p.wait(eng0, "AC", wdn_dv[og - 2])
                        p.dma(eng0, wsem, lambda e, og=og, wsl=wsl:
                              e.dma_start(out=WS(wsl), in_=wdn[og]))
                        p.wait("scalar", wsem, p.cnt[wsem])
                        p.op("scalar", lambda e, og=og, wsl=wsl: e.activation(
                            out=WS(wsl), in_=WS(wsl), func=AF.Abs,
                            accum_out=macc[:, 256 + og:256 + og + 1]))
                        wdn_dv[og] = p.cnt["AC"]
                p.wait("vector", sem, p.cnt[sem])
                p.wait("vector", "PE", p.cnt["PE"])
                p.op("vector", lambda e, sl=sl: e.tensor_tensor(
                    out=tmp[:], in0=GS(sl), in1=BC(0), op=ALU.subtract))
                sub_dv[g] = p.cnt["DV"]
                p.wait("tensor", "DV", sub_dv[g])
                for i4 in range(4):
                    b = 4 + i4 % 2
                    pe_use_bank(b)
                    for k in range(4):
                        i = i4 * 4 + k
                        p.op("tensor", lambda e, i=i, k=k, b=b: e.transpose(
                            out=pb[b][:, k * 128:(k + 1) * 128],
                            in_=tmp[:, i * 128:(i + 1) * 128], identity=eyef[:]),
                            inc="PE")
                    p.wait("vector", "PE", p.cnt["PE"])
                    p.op("vector", lambda e, i4=i4, g=g, b=b: e.tensor_reduce(
                        out=macc[:].rearrange("p (tb gg) -> p tb gg", gg=16)[
                            :, i4 * 4:(i4 + 1) * 4, g:g + 1],
                        in_=pb[b][:].rearrange("p (s f) -> p s f", f=128),
                        axis=AX.X, op=ALU.max, apply_absolute_value=True))
                    set_bank_free(b)
            p.op("vector", lambda e: e.tensor_reduce(
                out=cols[:, CMAXD:CMAXD + 16].rearrange("p (s o) -> p s o", o=1),
                in_=macc[:, 0:256].rearrange("p (tb gg) -> p tb gg", gg=16),
                axis=AX.X, op=ALU.max))

        def quant_rows(rstd_row, cs_row, phi_row, csh, bslot):
            row_from_cols(cols[:, CMAXD:CMAXD + 16], tmp[0:1, :], bslot)
            p.op("vector", lambda e: e.tensor_tensor(out=tmp[0:1, :], in0=tmp[0:1, :],
                                                     in1=rstd_row, op=ALU.mult))
            p.op("vector", lambda e: e.tensor_scalar(out=tmp[0:1, :], in0=tmp[0:1, :],
                                                     scalar1=1e-5, scalar2=None,
                                                     op0=ALU.max))
            p.wait("scalar", "DV", p.cnt["DV"])
            p.op("scalar", lambda e: e.activation(out=cs_row, in_=tmp[0:1, :],
                                                  func=AF.Ln))
            p.op("scalar", lambda e: e.activation(out=cs_row, in_=cs_row, func=AF.Exp,
                                                  scale=-1.0, bias=LN127))
            p.wait("vector", "AC", p.cnt["AC"])
            p.op("vector", lambda e: e.tensor_tensor(out=cs_row, in0=cs_row,
                                                     in1=rstd_row, op=ALU.mult))
            p.op("vector", lambda e, csh=csh: e.tensor_scalar(
                out=phi_row, in0=tmp[0:1, :], scalar1=cols[0:1, csh:csh + 1],
                scalar2=2.0 / 127.0, op0=ALU.mult, op1=ALU.mult))
            p.wait("scalar", "DV", p.cnt["DV"])
            p.wait("tensor", "DV", p.cnt["DV"])

        ln_maxd(x_chunk, ROW(R_M), do_wdn=True)
        quant_rows(ROW(R_RSTD), ROW(R_CS), ROW(R_PHI), CSH_GU, 0)
        p.barrier()
        if stage <= 1:
            return _finish(p, nc)

        def ln_quant(src, cs_row):
            bcast_row(cs_row, BC(1))
            qdv = {}
            for g in range(NG):
                sl = g % 2
                sem = f"LG{sl}"
                if g >= 2:
                    p.wait("sync", "DV", qdv[g - 2])
                p.dma("sync", sem, lambda e, g=g, sl=sl: e.dma_start(out=GS(sl),
                                                                     in_=src(g)))
                p.wait("vector", sem, p.cnt[sem])
                p.op("vector", lambda e, sl=sl: e.tensor_tensor(
                    out=tmp[:], in0=GS(sl), in1=BC(0), op=ALU.subtract))
                p.op("vector", lambda e: e.tensor_tensor(out=tmp[:], in0=tmp[:],
                                                         in1=BC(1), op=ALU.mult))
                p.op("vector", lambda e: e.tensor_scalar(out=tmp[:], in0=tmp[:],
                                                         scalar1=MAGIC, scalar2=None,
                                                         op0=ALU.add))
                p.op("vector", lambda e, g=g: e.tensor_scalar(
                    out=AR(g), in0=tmp[:], scalar1=MAGIC, scalar2=None,
                    op0=ALU.subtract))
                qdv[g] = p.cnt["DV"]

        ln_quant(x_chunk, ROW(R_CS))
        p.wait("vector", "AC", p.cnt["AC"])
        p.op("vector", lambda e: e.tensor_reduce(out=COL(CTMP1),
                                                 in_=macc[:, 256:320],
                                                 axis=AX.X, op=ALU.add))
        scalar_final(COL(CTMP1), CSH_DN, CNSH_DN, 8192.0 * 2048.0)
        p.barrier()
        if stage <= 2:
            return _finish(p, nc)

        def gu_era(first_use, phi_row):
            bcast_row(phi_row, BC(0))
            tern_dv, mm_pe, g_dv = {}, {}, {}
            pend = []
            tr_pe = {}

            def flush_pend():
                while pend:
                    (pj, pt, pgsl, pdv) = pend.pop(0)
                    btr = 4 + pt % 2
                    pe_use_bank(btr)
                    p.wait("tensor", "DV", pdv)
                    for k in range(4):
                        p.op("tensor", lambda e, pt=pt, k=k, btr=btr, pgsl=pgsl:
                             e.transpose(
                                 out=pb[btr][:, k * 128:(k + 1) * 128],
                                 in_=GS(pgsl)[:, pt * 512 + k * 128:
                                              pt * 512 + (k + 1) * 128],
                                 identity=eyef[:]), inc="PE")
                    tr_pe[pj] = p.cnt["PE"]
                    p.wait("vector", "PE", p.cnt["PE"])
                    p.op("vector", lambda e, pj=pj, pt=pt, btr=btr: e.tensor_reduce(
                        out=macc[:].rearrange("p (tb jj) -> p tb jj", jj=64)[
                            :, pt * 4:(pt + 1) * 4, pj:pj + 1],
                        in_=pb[btr][:].rearrange("p (s f) -> p s f", f=128),
                        axis=AX.X, op=ALU.max, apply_absolute_value=True))
                    set_bank_free(btr)

            def emit_tern(jj):
                wa, wb_ = (jj % 2) * 2, (jj % 2) * 2 + 1
                ta, tb_ = (jj % 2) * 2, (jj % 2) * 2 + 1
                if jj >= 2:
                    p.wait("sync", "DV", tern_dv[jj - 2])
                p.dma("sync", f"LW{wa}", lambda e, jj=jj, s=wa: e.dma_start(
                    out=WS(s), in_=wgu[jj]))
                p.dma("sync", f"LW{wb_}", lambda e, jj=jj, s=wb_: e.dma_start(
                    out=WS(s), in_=wgu[jj + 64]))
                p.wait("vector", f"LW{wa}", p.cnt[f"LW{wa}"])
                p.wait("vector", f"LW{wb_}", p.cnt[f"LW{wb_}"])
                if jj >= 2:
                    p.wait("vector", "PE", mm_pe[jj - 2])
                    p.wait("vector", f"TT{ta}", p.cnt[f"TT{ta}"])
                    p.wait("vector", f"TT{tb_}", p.cnt[f"TT{tb_}"])
                for (ws, tt) in ((wa, ta), (wb_, tb_)):
                    p.op("vector", lambda e, ws=ws: e.tensor_scalar(
                        out=bcst[:, 2048:4096], in0=WS(ws), scalar1=COL(CNSH_GU),
                        scalar2=None, op0=ALU.is_lt))
                    p.op("vector", lambda e, ws=ws, tt=tt: e.scalar_tensor_tensor(
                        out=TT(tt), in0=WS(ws), scalar=COL(CSH_GU),
                        in1=bcst[:, 2048:4096], op0=ALU.is_gt, op1=ALU.subtract))
                tern_dv[jj] = p.cnt["DV"]
                p.wait("scalar", "DV", tern_dv[jj])
                p.dma("scalar", f"TT{ta}", lambda e, jj=jj, tt=ta: e.dma_start(
                    out=terngu[jj], in_=TT(tt)))
                p.dma("scalar", f"TT{tb_}", lambda e, jj=jj, tt=tb_: e.dma_start(
                    out=terngu[jj + 64], in_=TT(tt)))

            if first_use:
                emit_tern(0)
            for j in range(NPAIR):
                wa, wb_ = (j % 2) * 2, (j % 2) * 2 + 1
                ta, tb_ = (j % 2) * 2, (j % 2) * 2 + 1
                if first_use:
                    if j + 1 < NPAIR:
                        emit_tern(j + 1)
                    p.wait("tensor", "DV", tern_dv[j])
                else:
                    if j >= 2:
                        p.wait("sync", "PE", mm_pe[j - 2])
                    p.dma("sync", f"LT{ta}", lambda e, j=j, s=ta: e.dma_start(
                        out=TT(s), in_=terngu[j]))
                    p.dma("sync", f"LT{tb_}", lambda e, j=j, s=tb_: e.dma_start(
                        out=TT(s), in_=terngu[j + 64]))
                    p.wait("tensor", f"LT{ta}", p.cnt[f"LT{ta}"])
                    p.wait("tensor", f"LT{tb_}", p.cnt[f"LT{tb_}"])
                gsl = j % 2
                for t in range(4):
                    ba, bb = t % 2, 2 + t % 2
                    pe_use_bank(ba)
                    for g in range(NG):
                        p.op("tensor", lambda e, g=g, t=t, tt=ta, ba=ba: e.matmul(
                            out=pb[ba][:], lhsT=TT(tt)[:, g * 128:(g + 1) * 128],
                            rhs=AR(g)[:, t * 512:(t + 1) * 512],
                            start=(g == 0), stop=(g == NG - 1)), inc="PE")
                    pe_a = p.cnt["PE"]
                    pe_use_bank(bb)
                    for g in range(NG):
                        p.op("tensor", lambda e, g=g, t=t, tt=tb_, bb=bb: e.matmul(
                            out=pb[bb][:], lhsT=TT(tt)[:, g * 128:(g + 1) * 128],
                            rhs=AR(g)[:, t * 512:(t + 1) * 512],
                            start=(g == 0), stop=(g == NG - 1)), inc="PE")
                    pe_b = p.cnt["PE"]
                    flush_pend()
                    if t == 0 and j >= 2:
                        p.wait("vector", f"TG{gsl}", p.cnt[f"TG{gsl}"])
                        p.wait("vector", "PE", tr_pe.get(j - 2, 0))
                    p.wait("vector", "PE", pe_a)
                    p.op("vector", lambda e, t=t, ba=ba: e.tensor_tensor(
                        out=tmp[:, t * 512:(t + 1) * 512], in0=pb[ba][:],
                        in1=BC(0)[:, t * 512:(t + 1) * 512], op=ALU.mult))
                    set_bank_free(ba)
                    p.wait("scalar", "DV", p.cnt["DV"])
                    p.op("scalar", lambda e, t=t: e.activation(
                        out=tmp[:, t * 512:(t + 1) * 512],
                        in_=tmp[:, t * 512:(t + 1) * 512], func=AF.Silu))
                    p.wait("vector", "PE", pe_b)
                    p.wait("vector", "AC", p.cnt["AC"])
                    p.op("vector", lambda e, t=t, bb=bb, gsl=gsl: e.tensor_tensor(
                        out=GS(gsl)[:, t * 512:(t + 1) * 512],
                        in0=tmp[:, t * 512:(t + 1) * 512], in1=pb[bb][:], op=ALU.mult))
                    set_bank_free(bb)
                    g_dv[(j, t)] = p.cnt["DV"]
                    pend.append((j, t, gsl, p.cnt["DV"]))
                mm_pe[j] = p.cnt["PE"]
                p.wait("scalar", "DV", g_dv[(j, 3)])
                for t in range(4):
                    p.dma("scalar", f"TG{gsl}", lambda e, j=j, t=t, gsl=gsl:
                          e.dma_start(out=g1d[t][:, j * 512:(j + 1) * 512],
                                      in_=GS(gsl)[:, t * 512:(t + 1) * 512]))
            flush_pend()

        gu_era(True, ROW(R_PHI))
        p.barrier()
        if stage <= 3:
            return _finish(p, nc)

        def g_rows_a(phi_row, q2_row, bslot):
            p.op("vector", lambda e: e.tensor_reduce(
                out=cols[:, CMAXD:CMAXD + 16].rearrange("p (s o) -> p s o", o=1),
                in_=macc[:].rearrange("p (tb jj) -> p tb jj", jj=64),
                axis=AX.X, op=ALU.max))
            row_from_cols(cols[:, CMAXD:CMAXD + 16], tmp[0:1, :], bslot)
            p.op("vector", lambda e: e.tensor_tensor(out=tmp[0:1, :], in0=tmp[0:1, :],
                                                     in1=phi_row, op=ALU.mult))
            p.op("vector", lambda e: e.tensor_scalar(out=ROW(R_CS), in0=tmp[0:1, :],
                                                     scalar1=1e-5, scalar2=None,
                                                     op0=ALU.max))
            p.wait("scalar", "DV", p.cnt["DV"])
            p.op("scalar", lambda e: e.activation(out=q2_row, in_=ROW(R_CS),
                                                  func=AF.Ln))
            p.op("scalar", lambda e: e.activation(out=q2_row, in_=q2_row, func=AF.Exp,
                                                  scale=-1.0, bias=LN127))
            p.wait("vector", "AC", p.cnt["AC"])
            p.op("vector", lambda e: e.tensor_tensor(out=q2_row, in0=q2_row,
                                                     in1=phi_row, op=ALU.mult))

        def f2_calc(f2_row):
            p.op("vector", lambda e: e.tensor_scalar(
                out=f2_row, in0=ROW(R_CS), scalar1=cols[0:1, CSH_DN:CSH_DN + 1],
                scalar2=2.0 / 127.0, op0=ALU.mult, op1=ALU.mult))
            p.wait("tensor", "DV", p.cnt["DV"])
            p.wait("scalar", "DV", p.cnt["DV"])

        g_rows_a(ROW(R_PHI), ROW(R_Q2), 1)
        f2_calc(ROW(R_F2))
        p.barrier()
        if stage <= 4:
            return _finish(p, nc)

        RS4 = lambda l: wbuf[:, 4096 + l * 512:4608 + l * 512]
        RSEMS = ("LR0", "LR1", "LS0", "LS1")
        def dn_era(first_use, q2_row, f2_row, res_src, dst4):
            bcast_row(q2_row, BC(0))
            bcast_row(f2_row, BC(1))
            gq_dv = {}
            t4_pe = {i: 0 for i in range(4)}
            t4_ld = {}
            tern_pe = {0: 0, 1: 0}
            wdnt_dv = {}
            res4_dv = {l: 0 for l in range(4)}
            res4_ld = {l: 0 for l in range(4)}
            for ti in range(4):
                first00 = first_use and ti == 0

                def emit_res(hq, ti=ti):
                    for l in range(4):
                        ci = hq * 4 + l
                        p.wait("sync", "DV", res4_dv[l])
                        p.dma("sync", RSEMS[l], lambda e, ci=ci, ti=ti, l=l:
                              e.dma_start(out=RS4(l), in_=res_src(ci, ti)))
                        res4_ld[l] = p.cnt[RSEMS[l]]

                def emit_t4(hq, b, ti=ti):
                    sl = b % 4
                    lsem = f"LT{sl}"
                    p.wait("sync", "PE", t4_pe[sl])
                    p.wait("sync", lsem, p.cnt[lsem])
                    if first_use:
                        p.wait("sync", "TT0", p.cnt["TT0"])
                        p.wait("sync", "TT1", p.cnt["TT1"])
                    p.dma("sync", lsem, lambda e, hq=hq, b=b, sl=sl: e.dma_start(
                        out=TT(sl), in_=terndn[hq][:, b * 2048:(b + 1) * 2048]))
                    t4_ld[(hq, b)] = p.cnt[lsem]

                def emit_mm4(hq, b, ti=ti):
                    sl = b % 4
                    bk = ((ti * 4 + hq) % 2) * 4
                    p.wait("tensor", f"LT{sl}", t4_ld[(hq, b)])
                    p.wait("tensor", "DV", gq_dv[b])
                    if b == 0:
                        for l in range(4):
                            pe_use_bank(bk + l)
                    for ko in range(4):
                        og = b * 4 + ko
                        for l in range(4):
                            p.op("tensor", lambda e, l=l, og=og, ko=ko, sl=sl,
                                 bb=bk: e.matmul(
                                     out=pb[bb + l][:],
                                     lhsT=TT(sl)[:, ko * 512 + l * 128:
                                                 ko * 512 + (l + 1) * 128],
                                     rhs=AR512(og),
                                     start=(og == 0), stop=(og == NOG - 1)),
                                 inc="PE")
                    t4_pe[sl] = p.cnt["PE"]

                def epilogue(hq, ti=ti):
                    pe_done = p.cnt["PE"]
                    bk = ((ti * 4 + hq) % 2) * 4
                    tsem = f"TS{hq % 2}"
                    eb = (hq % 2) * 2048
                    for l in range(4):
                        p.wait("vector", "PE", pe_done)
                        p.wait("vector", RSEMS[l], res4_ld[l])
                        if l == 0:
                            p.wait("vector", tsem, p.cnt[tsem])
                        p.op("vector", lambda e, l=l, ti=ti, b=bk + l: e.tensor_tensor(
                            out=tmp[:, 0:512], in0=pb[b][:],
                            in1=BC(1)[:, ti * 512:(ti + 1) * 512], op=ALU.mult))
                        set_bank_free(bk + l)
                        p.op("vector", lambda e, l=l, eb=eb: e.scalar_tensor_tensor(
                            out=rows2[:, eb + l * 512:eb + (l + 1) * 512],
                            in0=tmp[:, 0:512], scalar=0.5, in1=RS4(l),
                            op0=ALU.mult, op1=ALU.add))
                        res4_dv[l] = p.cnt["DV"]
                    p.wait("scalar", "DV", p.cnt["DV"])
                    p.wait("scalar", tsem, p.cnt[tsem])
                    p.dma("scalar", tsem, lambda e, hq=hq, ti=ti, eb=eb:
                          e.dma_start(out=dst4(hq, ti), in_=rows2[:, eb:eb + 2048]))

                p.wait("vector", "PE", p.cnt["PE"])
                for gb in range(NG):
                    sl = gb % 2
                    sem = f"LG{sl}"
                    if gb >= 2:
                        p.wait("sync", "DV", gq_dv[gb - 2])
                    p.dma("sync", sem, lambda e, gb=gb, sl=sl, ti=ti: e.dma_start(
                        out=GS(sl), in_=g1d[ti][:, gb * 2048:(gb + 1) * 2048]))
                    if not first00:
                        emit_t4(0, gb)
                        if gb == 13:
                            emit_res(0)
                    p.wait("vector", sem, p.cnt[sem])
                    for so in range(4):
                        p.op("vector", lambda e, sl=sl, ti=ti, so=so: e.tensor_tensor(
                            out=tmp[:, so * 512:(so + 1) * 512],
                            in0=GS(sl)[:, so * 512:(so + 1) * 512],
                            in1=BC(0)[:, ti * 512:(ti + 1) * 512], op=ALU.mult))
                    p.op("vector", lambda e: e.tensor_scalar(
                        out=tmp[:], in0=tmp[:], scalar1=MAGIC, scalar2=None,
                        op0=ALU.add))
                    p.op("vector", lambda e, gb=gb: e.tensor_scalar(
                        out=AR(gb), in0=tmp[:], scalar1=MAGIC, scalar2=None,
                        op0=ALU.subtract))
                    gq_dv[gb] = p.cnt["DV"]
                    if not first00:
                        emit_mm4(0, gb)
                if not first00:
                    epilogue(0)

                if first00:
                    emit_res(0)
                    for og in range(NOG):
                        tsl = og % 2
                        eng0 = "sync" if tsl == 0 else "gpsimd"
                        wsem = "LW0" if tsl == 0 else "LS9"
                        if og >= 2:
                            p.wait(eng0, "DV", wdnt_dv[og - 2])
                        p.dma(eng0, wsem, lambda e, og=og, sl=og % 2:
                              e.dma_start(out=WS(sl), in_=wdn[og]))
                        p.wait("vector", wsem, p.cnt[wsem])
                        p.wait("vector", "PE", tern_pe[tsl])
                        p.wait("vector", f"TT{tsl}", p.cnt[f"TT{tsl}"])
                        p.op("vector", lambda e, sl=og % 2: e.tensor_scalar(
                            out=tmp[:], in0=WS(sl), scalar1=COL(CNSH_DN),
                            scalar2=None, op0=ALU.is_lt))
                        p.op("vector", lambda e, sl=og % 2, tsl=tsl:
                             e.scalar_tensor_tensor(
                                 out=TT(tsl), in0=WS(sl), scalar=COL(CSH_DN),
                                 in1=tmp[:], op0=ALU.is_gt, op1=ALU.subtract))
                        wdnt_dv[og] = p.cnt["DV"]
                        p.wait("scalar", "DV", p.cnt["DV"])
                        p.dma("scalar", f"TT{tsl}",
                              lambda e, og=og, tsl=tsl: e.dma_start(
                                  out=terndn[:, :, og * 512:(og + 1) * 512]
                                  .rearrange("q p s -> p q s"),
                                  in_=TT(tsl)[:]))
                        p.wait("tensor", "DV", wdnt_dv[og])
                        p.wait("tensor", "DV", gq_dv[og // 4])
                        if og == 0:
                            for l in range(4):
                                pe_use_bank(l)
                        for l in range(4):
                            p.op("tensor", lambda e, l=l, og=og, tsl=tsl: e.matmul(
                                out=pb[l][:],
                                lhsT=TT(tsl)[:, l * 128:(l + 1) * 128],
                                rhs=AR512(og),
                                start=(og == 0), stop=(og == NOG - 1)), inc="PE")
                        tern_pe[tsl] = p.cnt["PE"]
                    for i in range(4):
                        t4_pe[i] = p.cnt["PE"]
                    epilogue(0)
                for hq in range(1, 4):
                    emit_t4(hq, 0)
                    emit_t4(hq, 1)
                    emit_t4(hq, 2)
                    emit_t4(hq, 3)
                    for b in range(NG):
                        if b + 4 < NG:
                            pass
                        emit_mm4(hq, b)
                        if b + 4 < NG:
                            emit_t4(hq, b + 4)
                        if b == 13:
                            emit_res(hq)
                    epilogue(hq)

        res_src1 = lambda ci, ti: x_cT[ci * 128:(ci + 1) * 128,
                                       ti * 512:(ti + 1) * 512]
        dst1 = lambda hq, ti: x1d[hq * 4:(hq + 1) * 4, :,
                                  ti * 512:(ti + 1) * 512].rearrange(
                                      "c p s -> p c s")
        dn_era(True, ROW(R_Q2), ROW(R_F2), res_src1, dst1)
        p.barrier()
        if stage <= 5:
            return _finish(p, nc)

        x1_chunk = lambda g: x1d[g]
        ln_stats(x1_chunk, ROW(R_M), ROW(R_RSTD))
        p.barrier()
        if stage <= 6:
            return _finish(p, nc)

        bcast_row(ROW(R_M), BC(0))
        bcast_row(ROW(R_RSTD), BC(1))
        zdv = {}
        for g in range(NG):
            sl = g % 2
            sem = f"LG{sl}"
            if g >= 2:
                p.wait("sync", "DV", zdv[g - 2])
            p.dma("sync", sem, lambda e, g=g, sl=sl: e.dma_start(out=GS(sl),
                                                                 in_=x1d[g]))
            p.wait("vector", sem, p.cnt[sem])
            p.op("vector", lambda e, sl=sl: e.tensor_tensor(
                out=tmp[:], in0=GS(sl), in1=BC(0), op=ALU.subtract))
            p.op("vector", lambda e, g=g: e.tensor_tensor(
                out=AR(g), in0=tmp[:], in1=BC(1), op=ALU.mult))
            zdv[g] = p.cnt["DV"]
            p.wait("gpsimd", "DV", zdv[g])
            p.dma("gpsimd", "TZ", lambda e, g=g: e.dma_start(out=zd[g], in_=AR(g)))
        p.barrier()
        if stage <= 7:
            return _finish(p, nc)

        ev_sg = {}

        def proj_era(tm_cbase, wmat, evac):
            mdv, mix_pl = {}, {}
            for g in range(NG):
                zsl = 2 + g % 2
                zsem = f"LT{zsl}"
                if g >= 2:
                    p.wait("sync", "DV", mdv[g - 2])
                p.dma("sync", zsem, lambda e, g=g, zsl=zsl: e.dma_start(out=TT(zsl),
                                                                        in_=zd[g]))
                p.wait("vector", zsem, p.cnt[zsem])
                p.op("vector", lambda e, zsl=zsl: e.tensor_tensor(
                    out=tmp[:, 1:2048], in0=TT(zsl)[:, 1:2048],
                    in1=TT(zsl)[:, 0:2047], op=ALU.subtract))
                p.op("vector", lambda e, zsl=zsl: e.tensor_copy(
                    out=tmp[:, 0:1], in_=TT(zsl)[:, 0:1]))
                p.op("vector", lambda e, g=g, zsl=zsl, cb=tm_cbase:
                     e.scalar_tensor_tensor(
                         out=AR(g)[:, 1:2048], in0=tmp[:, 1:2048],
                         scalar=cols[:, cb + g:cb + g + 1], in1=TT(zsl)[:, 0:2047],
                         op0=ALU.mult, op1=ALU.add))
                p.op("vector", lambda e, g=g, cb=tm_cbase: e.tensor_scalar(
                    out=AR(g)[:, 0:1], in0=tmp[:, 0:1],
                    scalar1=cols[:, cb + g:cb + g + 1], scalar2=None, op0=ALU.mult))
                mdv[g] = p.cnt["DV"]
            p.wait("tensor", "DV", p.cnt["DV"])
            p.wait("tensor", "PL", p.cnt["PL"])
            mixpl_total = p.cnt["PL"]
            wconv, wmm_pe = {}, {}
            NSL = 2 if evac == "r" else 4

            def emit_conv(dc):
                wsl = dc % 2
                tsl = dc % NSL
                wsem = f"LW{wsl}"
                if dc >= 2:
                    p.wait("sync", "DV", wconv[dc - 2])
                p.dma("sync", wsem, lambda e, dc=dc, wsl=wsl: e.dma_start(
                    out=WS(wsl), in_=wmat[dc]))
                p.wait("vector", wsem, p.cnt[wsem])
                p.wait("vector", "PL", mixpl_total)
                if dc >= NSL:
                    p.wait("vector", "PE", wmm_pe[dc - NSL])
                p.op("vector", lambda e, tsl=tsl: e.tensor_copy(out=TT(tsl),
                                                                in_=WS(wsl)))
                wconv[dc] = p.cnt["DV"]

            ST_A = (WS(3), rows[:, 0:2048])
            ST_V = (GS(1), rows[:, 2048:4096])
            st_av, st_bv = GS(0), WS(2)
            t_ew = macc[:, 0:1024]
            t_nacc2 = bcst[:, 0:2048]
            t_den2 = bcst[:, 2048:4096]
            t_dacc2 = rows2[:, 0:2048]
            t_num2 = rows2[:, 2048:4096]
            t_rec2 = t_dacc2
            t_wkv2 = t_nacc2
            scan_it_dv = {}
            scan_nacc_dv = {}
            scan_dacc_dv = {}
            scan_num_dv = {}
            scan_ac = {}
            scan_pl_bv = {}
            scan_pl_av = {}
            rq_tr = {}

            mix_pl0 = p.cnt["PL"]

            def emit_scan_loads(dg):
                par = dg % 2
                sa, sv = (f"LS{par * 2}", f"LS{par * 2 + 1}")
                p.wait("sync", "DV", scan_it_dv.get(dg - 2, 0))
                p.wait("sync", "AC", scan_ac.get(dg - 2, 0))
                p.wait("sync", "PL", scan_pl_bv.get(dg - 2, mix_pl0))
                p.wait("sync", sa, p.cnt[sa])
                p.dma("sync", sa, lambda e, dg=dg, par=par: e.dma_start(
                    out=ST_A[par], in_=kvsp[dg]))
                p.wait("sync", sv, p.cnt[sv])
                p.dma("sync", sv, lambda e, dg=dg, par=par: e.dma_start(
                    out=ST_V[par], in_=vsp[dg]))
                p.wait("gpsimd", sa, p.cnt[sa])
                p.wait("gpsimd", sv, p.cnt[sv])
                p.wait("gpsimd", "DV", scan_nacc_dv.get(dg - 1, 0))
                p.op("gpsimd", lambda e, par=par: e.tensor_tensor(
                    out=st_av, in0=ST_A[par], in1=ST_V[par], op=ALU.mult))
                scan_pl_av[dg] = p.cnt["PL"]

            def emit_scan(dg):
                par = dg % 2
                sa, sv = (f"LS{par * 2}", f"LS{par * 2 + 1}")
                st_a, st_v = ST_A[par], ST_V[par]
                st_b = st_a
                p.wait("scalar", "DV", scan_dacc_dv.get(dg - 1, 0))
                p.op("scalar", lambda e, dg=dg: e.activation(
                    out=t_ew, in_=tmp[:, 0:1024], func=AF.Identity, scale=0.0,
                    bias=cols[:, CEW + dg:CEW + dg + 1]))
                ew_ac = p.cnt["AC"]
                p.wait("vector", "AC", ew_ac)
                p.wait("vector", "PL", scan_pl_av[dg])
                p.op("vector", lambda e: e.tensor_tensor_scan(
                    out=t_nacc2[:, 0:1024], data0=t_ew, data1=st_av[:, 0:1024],
                    initial=0.0, op0=ALU.mult, op1=ALU.add))
                p.op("vector", lambda e: e.tensor_tensor_scan(
                    out=t_nacc2[:, 1024:2048], data0=t_ew, data1=st_av[:, 1024:2048],
                    initial=t_nacc2[:, 1023:1024], op0=ALU.mult, op1=ALU.add))
                scan_nacc_dv[dg] = p.cnt["DV"]
                p.wait("vector", sa, p.cnt[sa])
                p.op("vector", lambda e, st_a=st_a: e.tensor_tensor_scan(
                    out=t_dacc2[:, 0:1024], data0=t_ew, data1=st_a[:, 0:1024],
                    initial=0.0, op0=ALU.mult, op1=ALU.add))
                p.op("vector", lambda e, st_a=st_a: e.tensor_tensor_scan(
                    out=t_dacc2[:, 1024:2048], data0=t_ew, data1=st_a[:, 1024:2048],
                    initial=t_dacc2[:, 1023:1024], op0=ALU.mult, op1=ALU.add))
                scan_dacc_dv[dg] = p.cnt["DV"]
                p.wait("scalar", "DV", scan_dacc_dv[dg])
                p.wait("scalar", "PL", scan_pl_av[dg])
                p.op("scalar", lambda e, dg=dg, st_b=st_b, st_a=st_a: e.activation(
                    out=st_b, in_=st_a, func=AF.Identity,
                    scale=cols[:, 112 + dg:112 + dg + 1]))
                scan_ac[dg] = p.cnt["AC"]
                p.wait("gpsimd", "AC", scan_ac[dg])
                p.wait("gpsimd", "DV", scan_num_dv.get(dg - 1, 0))
                p.op("gpsimd", lambda e, st_b=st_b, st_v=st_v: e.tensor_tensor(
                    out=st_bv, in0=st_b, in1=st_v, op=ALU.mult))
                scan_pl_bv[dg] = p.cnt["PL"]
                p.wait("vector", "AC", scan_ac[dg])
                p.op("vector", lambda e, st_b=st_b: e.tensor_tensor(
                    out=t_den2[:, 1:2048], in0=t_dacc2[:, 0:2047],
                    in1=st_b[:, 1:2048], op=ALU.add))
                p.op("vector", lambda e, st_b=st_b: e.tensor_copy(
                    out=t_den2[:, 0:1], in_=st_b[:, 0:1]))
                den_dv = p.cnt["DV"]
                p.wait("vector", "PL", scan_pl_bv[dg])
                p.op("vector", lambda e: e.tensor_tensor(
                    out=t_num2[:, 1:2048], in0=t_nacc2[:, 0:2047],
                    in1=st_bv[:, 1:2048], op=ALU.add))
                p.op("vector", lambda e: e.tensor_copy(
                    out=t_num2[:, 0:1], in_=st_bv[:, 0:1]))
                scan_num_dv[dg] = p.cnt["DV"]
                p.wait("scalar", "DV", den_dv)
                p.op("scalar", lambda e: e.activation(out=t_rec2, in_=t_den2,
                                                      func=AF.Ln))
                p.op("scalar", lambda e: e.activation(out=t_rec2, in_=t_rec2,
                                                      func=AF.Exp, scale=-1.0))
                rec_ac = p.cnt["AC"]
                p.wait("vector", "AC", rec_ac)
                p.op("vector", lambda e: e.tensor_tensor(
                    out=t_wkv2, in0=t_num2, in1=t_rec2, op=ALU.mult))
                rqs = 2 + par
                if dg < 2:
                    p.wait("vector", f"LT{rqs}", p.cnt[f"LT{rqs}"])
                p.wait("vector", f"TR{par}", rq_tr.get(dg - 2, 0))
                p.op("vector", lambda e, rqs=rqs: e.tensor_tensor(
                    out=TT(rqs), in0=t_wkv2, in1=tmp[:], op=ALU.mult))
                scan_it_dv[dg] = p.cnt["DV"]
                p.wait("scalar", "DV", scan_it_dv[dg])
                p.wait("scalar", f"TR{par}", p.cnt[f"TR{par}"])
                p.dma("scalar", f"TR{par}", lambda e, dg=dg, rqs=rqs: e.dma_start(
                    out=zd[dg], in_=TT(rqs)))
                rq_tr[dg] = p.cnt[f"TR{par}"]

            for i0 in range(NSL - 1):
                emit_conv(i0)
            for dc in range(NG):
                wsl = dc % NSL
                if dc + NSL - 1 < NG:
                    emit_conv(dc + NSL - 1)
                if evac == "r":
                    if dc >= 1:
                        emit_scan(dc - 1)
                    emit_scan_loads(dc)
                p.wait("tensor", "DV", wconv[dc])
                for t in range(4):
                    b = (dc % 2) * 4 + t
                    pe_use_bank(b)
                    for g in range(NG):
                        p.op("tensor", lambda e, g=g, t=t, wsl=wsl, b=b: e.matmul(
                            out=pb[b][:], lhsT=TT(wsl)[:, g * 128:(g + 1) * 128],
                            rhs=AR(g)[:, t * 512:(t + 1) * 512],
                            start=(g == 0), stop=(g == NG - 1)), inc="PE")
                    peg = p.cnt["PE"]
                    if evac == "v":
                        stg = tmp if dc % 2 == 0 else rows2
                        tsem = f"TS{dc % 2}"
                        p.wait("vector", "PE", peg)
                        if t == 0:
                            p.wait("vector", tsem, p.cnt[tsem])
                        p.op("vector", lambda e, b=b, t=t, stg=stg: e.tensor_copy(
                            out=stg[:, t * 512:(t + 1) * 512], in_=pb[b][:]))
                        set_bank_free(b)
                        if t == 3:
                            p.wait("scalar", "DV", p.cnt["DV"])
                            p.wait("scalar", tsem, p.cnt[tsem])
                            p.dma("scalar", tsem, lambda e, dc=dc, stg=stg:
                                  e.dma_start(out=vsp[dc], in_=stg[:, 0:2048]))
                    elif evac == "k":
                        stg = tmp if dc % 2 == 0 else rows2
                        ksem = f"TS{dc % 2}"
                        p.wait("scalar", "PE", peg)
                        if t == 0:
                            p.wait("scalar", ksem, p.cnt[ksem])
                        p.op("scalar", lambda e, b=b, t=t, stg=stg: e.activation(
                            out=stg[:, t * 512:(t + 1) * 512], in_=pb[b][:],
                            func=AF.Exp))
                        set_bank_free(b, "AC")
                        if t == 3:
                            p.wait("scalar", "AC", p.cnt["AC"])
                            p.dma("scalar", ksem, lambda e, dc=dc, stg=stg:
                                  e.dma_start(out=kvsp[dc], in_=stg[:, 0:2048]))
                    else:
                        p.wait("scalar", "PE", peg)
                        p.wait("scalar", "DV", scan_it_dv.get(dc - 1, 0))
                        p.op("scalar", lambda e, b=b, t=t: e.activation(
                            out=tmp[:, t * 512:(t + 1) * 512], in_=pb[b][:],
                            func=AF.Sigmoid))
                        set_bank_free(b, "AC")
                wmm_pe[dc] = p.cnt["PE"]
            if evac == "r":
                emit_scan(NG - 1)
            p.barrier()

        proj_era(CTMV, wv, "v")
        if stage <= 8:
            return _finish(p, nc)
        proj_era(CTMK, wk, "k")
        if stage <= 9:
            return _finish(p, nc)
        proj_era(CTMR, wr, "r")
        if stage <= 10:
            return _finish(p, nc)

        rql = {}
        for dg in range(NG):
            sem = f"LT{dg % 4}"
            p.wait("sync", sem, p.cnt[sem])
            p.dma("sync", sem, lambda e, dg=dg: e.dma_start(out=AR(dg), in_=zd[dg]))
            rql[dg] = (sem, p.cnt[sem])
        for dg in range(NG):
            p.wait("tensor", rql[dg][0], rql[dg][1])
        if stage <= 11:
            return _finish(p, nc)

        wo_conv, wo_pe, x1dv = {}, {}, {}

        def emit_woconv(hc):
            wsl = hc % 2
            tsl = hc % 4
            wsem = f"LW{wsl}"
            if hc >= 2:
                p.wait("sync", "DV", wo_conv[hc - 2])
            p.dma("sync", wsem, lambda e, hc=hc, wsl=wsl: e.dma_start(out=WS(wsl),
                                                                      in_=wo[hc]))
            p.wait("vector", wsem, p.cnt[wsem])
            if hc >= 4:
                p.wait("vector", "PE", wo_pe[hc - 4])
            p.op("vector", lambda e, tsl=tsl: e.tensor_copy(out=TT(tsl), in_=WS(wsl)))
            wo_conv[hc] = p.cnt["DV"]

        emit_woconv(0)
        emit_woconv(1)
        emit_woconv(2)
        for hc in range(NG):
            wsl = hc % 4
            if hc + 3 < NG:
                emit_woconv(hc + 3)
            p.wait("tensor", "DV", wo_conv[hc])
            for t in range(4):
                b = (hc % 2) * 4 + t
                pe_use_bank(b)
                for g in range(NG):
                    p.op("tensor", lambda e, g=g, t=t, wsl=wsl, b=b: e.matmul(
                        out=pb[b][:], lhsT=TT(wsl)[:, g * 128:(g + 1) * 128],
                        rhs=AR(g)[:, t * 512:(t + 1) * 512],
                        start=(g == 0), stop=(g == NG - 1)), inc="PE")
                peg = p.cnt["PE"]
                rsl = t % 2
                p.wait("sync", "DV", x1dv.get(rsl, 0))
                p.dma("sync", f"LR{rsl}", lambda e, hc=hc, t=t, rsl=rsl: e.dma_start(
                    out=RS(rsl), in_=x1d[hc][:, t * 512:(t + 1) * 512]))
                ldx = p.cnt[f"LR{rsl}"]
                stg = tmp if hc % 2 == 0 else rows2
                tsem = f"TS{hc % 2}"
                p.wait("vector", "PE", peg)
                p.wait("vector", f"LR{rsl}", ldx)
                if t == 0:
                    p.wait("vector", tsem, p.cnt[tsem])
                p.op("vector", lambda e, t=t, b=b, rsl=rsl, stg=stg: e.tensor_tensor(
                    out=stg[:, t * 512:(t + 1) * 512], in0=pb[b][:], in1=RS(rsl),
                    op=ALU.add))
                set_bank_free(b)
                x1dv[rsl] = p.cnt["DV"]
                if t == 3:
                    p.wait("scalar", "DV", p.cnt["DV"])
                    p.wait("scalar", tsem, p.cnt[tsem])
                    p.dma("scalar", tsem, lambda e, hc=hc, stg=stg: e.dma_start(
                        out=x2d[hc], in_=stg[:, 0:2048]))
            wo_pe[hc] = p.cnt["PE"]
        p.barrier()
        if stage <= 12:
            return _finish(p, nc)

        x2_chunk = lambda g: x2d[g]
        ln_stats(x2_chunk, ROW(R_M), ROW(R_RSTD))
        p.barrier()
        ln_maxd(x2_chunk, ROW(R_M))
        quant_rows(ROW(R_RSTD), ROW(R_CS), ROW(R_PHI), CSH_GU, 2)
        p.barrier()
        ln_quant(x2_chunk, ROW(R_CS))
        p.barrier()
        if stage <= 13:
            return _finish(p, nc)

        gu_era(False, ROW(R_PHI))
        p.barrier()
        if stage <= 14:
            return _finish(p, nc)

        g_rows_a(ROW(R_PHI), ROW(R_Q2), 3)
        f2_calc(ROW(R_F2))
        p.barrier()
        if stage <= 15:
            return _finish(p, nc)

        res_src2 = lambda ci, ti: x2d[ci][:, ti * 512:(ti + 1) * 512]
        dst2 = lambda hq, ti: out_cT[hq * 4:(hq + 1) * 4, :,
                                     ti * 512:(ti + 1) * 512].rearrange(
                                         "c p s -> p c s")
        dn_era(False, ROW(R_Q2), ROW(R_F2), res_src2, dst2)
        return _finish(p, nc)


def _finish(p, nc):
    snap = dict(p.cnt)
    for eng in ENGINES:
        for s, v in snap.items():
            p.wait(eng, s, v)
    with nc.Block() as block:
        for eng in ENGINES:
            def body(e, eng=eng):
                for f in p.q[eng]:
                    f(e)
            getattr(block, eng)(body)
    return nc


def _host_prep(inputs):
    import ml_dtypes
    d = {}
    A = inputs["L0_ffn_w_gate_up"].reshape(128, 128, 16, 128)
    d["wgu"] = np.ascontiguousarray(A.transpose(0, 3, 2, 1)).reshape(128, 128, 2048)
    A = inputs["L0_ffn_w_down"].reshape(16, 128, 64, 128)
    d["wdn"] = np.ascontiguousarray(A.transpose(2, 3, 0, 1)).reshape(64, 128, 2048)
    for nm, W in (("wk", inputs["L0_attn_Wk"]), ("wv", inputs["L0_attn_Wv"]),
                  ("wr", inputs["L0_attn_Wr"]), ("wo", inputs["L0_attn_Wo"])):
        A = W.reshape(16, 128, 16, 128)
        d[nm] = np.ascontiguousarray(A.transpose(0, 3, 2, 1)).reshape(16, 128, 2048)
    tv = np.stack([np.asarray(inputs[k], np.float32).reshape(16, 128) for k in (
        "L0_attn_time_decay", "L0_attn_time_first", "L0_attn_time_mix_key",
        "L0_attn_time_mix_value", "L0_attn_time_mix_receptance")])
    d["tvecs"] = np.ascontiguousarray(tv)
    d["eyef"] = np.eye(128, dtype=np.float32)
    consts = np.zeros((128, 128), np.float32)
    consts[0, :] = 1.0
    consts[:, 0] = 1.0
    consts[:, 1] = 1e-5
    consts[:, 2] = LN127
    consts[0, 0] = 1.0
    d["consts"] = consts
    d["constsb"] = consts.astype(ml_dtypes.bfloat16)
    d["onesrow"] = np.ones((1, 128), np.float32)
    return d


def kernel(**inputs):
    hidden = np.asarray(inputs["L0_attn_hidden"], np.float32)
    shared = _host_prep({k: np.asarray(v, np.float32) for k, v in inputs.items()
                         if k != "L0_attn_hidden"})
    nc = build()
    in_maps = []
    for b in range(8):
        m = dict(shared)
        m["x_cT"] = np.ascontiguousarray(hidden[b].T)
        in_maps.append(m)
    res = run_bass_kernel_spmd(nc, in_maps, core_ids=list(range(8)))
    out = np.stack([np.ascontiguousarray(
        np.asarray(r["out_cT"]).reshape(2048, 2048).T) for r in res.results])
    return out.astype(np.float32)


# version 22
# speedup vs baseline: 1.0349x; 1.0349x over previous
import numpy as np
from contextlib import ExitStack

import concourse.bass as bass
import concourse.mybir as mybir
from concourse.bass_utils import run_bass_kernel_spmd

F32 = mybir.dt.float32
BF16 = mybir.dt.bfloat16
AF = mybir.ActivationFunctionType
ALU = mybir.AluOpType
AX = mybir.AxisListType

NG = 16
NPAIR = 64
NOG = 64
MAGIC = 12582912.0
LN127 = float(np.log(127.0))
FULL = 99

ENGINES = ("sync", "tensor", "vector", "scalar", "gpsimd")
OWN = {"vector": "DV", "scalar": "AC", "gpsimd": "PL"}
DMA_SEMS = (["LC", "LV", "LB", "TB", "TZ"]
            + [f"LW{i}" for i in range(4)] + ["LG0", "LG1"]
            + [f"LT{i}" for i in range(4)] + ["LR0", "LR1"]
            + [f"LS{i}" for i in range(10)]
            + [f"TT{i}" for i in range(4)] + ["TG0", "TG1", "TR0", "TR1"]
            + [f"TS{i}" for i in range(8)] + [f"TA{i}" for i in range(8)])


class Prog:
    def __init__(self):
        self.q = {e: [] for e in ENGINES}
        self.cnt = {}
        self.last_wait = {e: {} for e in ENGINES}
        self.sems = {}

    def new_sem(self, name):
        self.cnt[name] = 0

    def wait(self, eng, sem, val=None):
        if val is None:
            val = self.cnt[sem]
        if val <= 0 or self.last_wait[eng].get(sem, 0) >= val:
            return
        self.last_wait[eng][sem] = val
        self.q[eng].append(lambda e, s=sem, v=val: e.wait_ge(self.sems[s], v))

    def op(self, eng, fn, inc=None, n=1, dma=False):
        if not dma and eng in OWN:
            self.wait(eng, OWN[eng])
            if inc is None:
                inc = OWN[eng]
        if inc is None:
            self.q[eng].append(fn)
            return None
        self.cnt[inc] += n
        tgt = self.cnt[inc]

        def run(e, f=fn, s=inc, k=n):
            f(e).then_inc(self.sems[s], k)
        self.q[eng].append(run)
        return tgt

    def dma(self, eng, sem, fn):
        return self.op(eng, fn, inc=sem, n=16, dma=True)

    def barrier(self):
        snap = dict(self.cnt)
        for eng in ENGINES:
            for s, v in snap.items():
                self.wait(eng, s, v)


def build(stage=FULL, dbg=False):
    nc = bass.Bass("TRN2", target_bir_lowering=False)
    kscr = "ExternalOutput" if dbg else "Internal"

    x_cT = nc.dram_tensor("x_cT", [2048, 2048], F32, kind="ExternalInput").ap()
    wgu = nc.dram_tensor("wgu", [128, 128, 2048], F32, kind="ExternalInput").ap()
    wdn = nc.dram_tensor("wdn", [64, 128, 2048], F32, kind="ExternalInput").ap()
    wk = nc.dram_tensor("wk", [16, 128, 2048], F32, kind="ExternalInput").ap()
    wv = nc.dram_tensor("wv", [16, 128, 2048], F32, kind="ExternalInput").ap()
    wr = nc.dram_tensor("wr", [16, 128, 2048], F32, kind="ExternalInput").ap()
    wo = nc.dram_tensor("wo", [16, 128, 2048], F32, kind="ExternalInput").ap()
    tvecs = nc.dram_tensor("tvecs", [5, 16, 128], F32, kind="ExternalInput").ap()
    eyef_in = nc.dram_tensor("eyef", [128, 128], F32, kind="ExternalInput").ap()
    consts_in = nc.dram_tensor("consts", [128, 128], F32, kind="ExternalInput").ap()
    constsb_in = nc.dram_tensor("constsb", [128, 128], BF16, kind="ExternalInput").ap()
    onesrow_in = nc.dram_tensor("onesrow", [1, 128], F32, kind="ExternalInput").ap()

    out_cT = nc.dram_tensor("out_cT", [16, 128, 2048], F32, kind="ExternalOutput").ap()

    x1d = nc.dram_tensor("x1d", [16, 128, 2048], F32, kind=kscr).ap()
    x2d = nc.dram_tensor("x2d", [16, 128, 2048], F32, kind=kscr).ap()
    g1d = nc.dram_tensor("g1d", [4, 128, 32768], F32, kind=kscr).ap()
    zd = nc.dram_tensor("zd", [16, 128, 2048], BF16, kind=kscr).ap()
    terngu = nc.dram_tensor("terngu", [128, 128, 2048], BF16, kind="Internal").ap()
    terndn = nc.dram_tensor("terndn", [4, 128, 32768], BF16, kind="Internal").ap()
    vsp = nc.dram_tensor("vsp", [16, 128, 2048], F32, kind="Internal").ap()
    kvsp = nc.dram_tensor("kvsp", [16, 128, 2048], F32, kind=kscr).ap()
    srsp = nc.dram_tensor("srsp", [16, 128, 2048], F32, kind=kscr).ap()
    bounce = nc.dram_tensor("bounce", [4, 2048], F32, kind="Internal").ap()

    p = Prog()
    for s in ("PE", "DV", "AC", "PL"):
        p.new_sem(s)
    for s in DMA_SEMS:
        p.new_sem(s)

    with ExitStack() as stack:
        en = stack.enter_context
        for s in list(p.cnt):
            p.sems[s] = en(nc.semaphore(s))

        bfA = en(nc.sbuf_tensor("bfA", [128, 32768], BF16))
        wbuf = en(nc.sbuf_tensor("wbuf", [128, 8192], F32))
        gsb = en(nc.sbuf_tensor("gsb", [128, 4096], F32))
        tern = en(nc.sbuf_tensor("tern", [128, 8192], BF16))
        bcst = en(nc.sbuf_tensor("bcst", [128, 4096], F32))
        rows = en(nc.sbuf_tensor("rows", [128, 4096], F32))
        rows2 = en(nc.sbuf_tensor("rows2", [128, 4096], F32))
        macc = en(nc.sbuf_tensor("macc", [128, 1024], F32))
        tmp = en(nc.sbuf_tensor("tmp", [128, 2048], F32))
        cols = en(nc.sbuf_tensor("cols", [128, 128], F32))
        eyef = en(nc.sbuf_tensor("eyef_s", [128, 128], F32))
        consts = en(nc.sbuf_tensor("consts_s", [128, 128], F32))
        constsb = en(nc.sbuf_tensor("constsb_s", [128, 128], BF16))
        vstage = en(nc.sbuf_tensor("vstage", [16, 128], F32))
        onesr = en(nc.sbuf_tensor("onesr", [1, 128], F32))
        pb = [en(nc.psum_tensor(f"pb{i}", [128, 512], F32)) for i in range(8)]
        nc.const_aps.register(F32, 1e-5, consts[:, 1:2])
        nc.const_aps.register(F32, LN127, consts[:, 2:3])

        WS = lambda i: wbuf[:, i * 2048:(i + 1) * 2048]
        GS = lambda i: gsb[:, i * 2048:(i + 1) * 2048]
        TT = lambda i: tern[:, i * 2048:(i + 1) * 2048]
        BC = lambda i: bcst[:, i * 2048:(i + 1) * 2048]
        AR = lambda g: bfA[:, g * 2048:(g + 1) * 2048]
        AR512 = lambda i: bfA[:, i * 512:(i + 1) * 512]
        RS = lambda i: gsb[:, 1024 + i * 2048:1536 + i * 2048]

        def ROW(i):
            i = i % 4
            t_ = rows if i < 2 else rows2
            c0 = (i % 2) * 2048
            return t_[0:1, c0:c0 + 2048]

        R_M, R_RSTD, R_CS, R_PHI, R_Q2, R_F2 = 0, 1, 2, 3, 4, 5
        COL = lambda i: cols[:, i:i + 1]
        CEW, CU, CTMK, CTMV, CTMR = 0, 16, 32, 48, 64
        CSH_GU, CNSH_GU, CSH_DN, CNSH_DN = 80, 81, 82, 83
        CTMP0, CTMP1 = 84, 85
        CCAR = 88
        CMAXD = 96
        onesrow_f = onesr[0:1, :]
        onescol_f = consts[:, 0:1]
        onescol_b = constsb[:, 0:1]

        bank_free = [(None, 0)] * 8

        def pe_use_bank(b):
            sem, v = bank_free[b]
            if sem is not None:
                p.wait("tensor", sem, v)

        def set_bank_free(b, sem="DV"):
            bank_free[b] = (sem, p.cnt[sem])

        def bcast_row(row_ap, dst, banks=(6, 7)):
            for i in range(4):
                b = banks[i % 2]
                pe_use_bank(b)
                p.wait("tensor", "DV", p.cnt["DV"])
                p.op("tensor", lambda e, i=i, b=b: e.matmul(
                    out=pb[b][:], lhsT=onesrow_f, rhs=row_ap[:, i * 512:(i + 1) * 512],
                    start=True, stop=True), inc="PE")
                p.wait("vector", "PE", p.cnt["PE"])
                p.op("vector", lambda e, i=i, b=b: e.tensor_copy(
                    out=dst[:, i * 512:(i + 1) * 512], in_=pb[b][:]))
                set_bank_free(b)

        def row_from_cols(col16, dst_row, bslot, bank=6):
            pe_use_bank(bank)
            p.wait("tensor", "DV", p.cnt["DV"])
            p.op("tensor", lambda e: e.transpose(out=pb[bank][0:16, 0:128], in_=col16,
                                                 identity=eyef[:]), inc="PE")
            p.wait("vector", "PE", p.cnt["PE"])
            p.wait("vector", "TB", p.cnt["TB"])
            p.op("vector", lambda e: e.tensor_copy(out=vstage[:],
                                                   in_=pb[bank][0:16, 0:128]))
            set_bank_free(bank)
            p.wait("gpsimd", "DV", p.cnt["DV"])
            p.dma("gpsimd", "TB", lambda e: e.dma_start(
                out=bounce[bslot:bslot + 1, :].rearrange("o (a b) -> (o a) b", a=16),
                in_=vstage[:]))
            p.wait("sync", "TB", p.cnt["TB"])
            p.wait("sync", "DV", p.cnt["DV"])
            p.dma("sync", "LB", lambda e: e.dma_start(out=dst_row,
                                                      in_=bounce[bslot:bslot + 1, :]))
            for eng in ("vector", "scalar", "tensor"):
                p.wait(eng, "LB", p.cnt["LB"])

        def scalar_final(sum_col_ap, csh, cnsh, nelem, banks=(6, 7)):
            b0, b1 = banks
            pe_use_bank(b0)
            p.wait("tensor", "DV", p.cnt["DV"])
            p.op("tensor", lambda e, b0=b0: e.matmul(
                out=pb[b0][0:1, 0:1], lhsT=onescol_f,
                rhs=sum_col_ap, start=True, stop=True), inc="PE")
            p.wait("vector", "PE", p.cnt["PE"])
            p.op("vector", lambda e, b0=b0: e.tensor_scalar(
                out=cols[0:1, CTMP0:CTMP0 + 1], in0=pb[b0][0:1, 0:1],
                scalar1=1.0 / nelem, scalar2=1e-5, op0=ALU.mult, op1=ALU.max))
            set_bank_free(b0)
            pe_use_bank(b1)
            p.wait("tensor", "DV", p.cnt["DV"])
            p.op("tensor", lambda e, b1=b1: e.matmul(
                out=pb[b1][:, 0:1], lhsT=onesrow_f,
                rhs=cols[0:1, CTMP0:CTMP0 + 1],
                start=True, stop=True), inc="PE")
            p.wait("vector", "PE", p.cnt["PE"])
            p.op("vector", lambda e, csh=csh, b1=b1: e.tensor_scalar(
                out=COL(csh), in0=pb[b1][:, 0:1], scalar1=0.5, scalar2=None,
                op0=ALU.mult))
            p.op("vector", lambda e, cnsh=cnsh, b1=b1: e.tensor_scalar(
                out=COL(cnsh), in0=pb[b1][:, 0:1], scalar1=-0.5, scalar2=None,
                op0=ALU.mult))
            set_bank_free(b1)

        p.dma("sync", "LC", lambda e: e.dma_start(out=eyef[:], in_=eyef_in))
        p.dma("sync", "LC", lambda e: e.dma_start(out=consts[:], in_=consts_in))
        p.dma("sync", "LC", lambda e: e.dma_start(out=constsb[:], in_=constsb_in))
        p.dma("sync", "LC", lambda e: e.dma_start(out=onesr[:], in_=onesrow_in))
        for eng in ("tensor", "vector", "scalar", "gpsimd"):
            p.wait(eng, "LC", p.cnt["LC"])

        for vi, cb in ((0, CEW), (1, CU), (2, CTMK), (3, CTMV), (4, CTMR)):
            p.dma("sync", "LV", lambda e, vi=vi: e.dma_start(out=vstage[:],
                                                             in_=tvecs[vi]))
            p.wait("tensor", "LV", p.cnt["LV"])
            pe_use_bank(6)
            p.op("tensor", lambda e: e.transpose(out=pb[6][0:128, 0:16], in_=vstage[:],
                                                 identity=eyef[0:16, 0:16]), inc="PE")
            p.wait("vector", "PE", p.cnt["PE"])
            p.op("vector", lambda e, cb=cb: e.tensor_copy(out=cols[:, cb:cb + 16],
                                                          in_=pb[6][0:128, 0:16]))
            set_bank_free(6)
            p.wait("sync", "PE", p.cnt["PE"])
        p.wait("scalar", "DV", p.cnt["DV"])
        p.op("scalar", lambda e: e.activation(out=cols[:, CEW:CEW + 16],
                                              in_=cols[:, CEW:CEW + 16], func=AF.Exp))
        p.op("scalar", lambda e: e.activation(out=cols[:, CEW:CEW + 16],
                                              in_=cols[:, CEW:CEW + 16], func=AF.Exp,
                                              scale=-1.0))
        p.op("scalar", lambda e: e.activation(out=cols[:, 112:128],
                                              in_=cols[:, CU:CU + 16], func=AF.Exp))
        p.wait("vector", "AC", p.cnt["AC"])
        p.wait("gpsimd", "AC", p.cnt["AC"])

        x_chunk = lambda g: x_cT[g * 128:(g + 1) * 128, :]
        wpart = {}
        ln_cdv, ln_cac = {}, {}
        for b in (4, 5, 6, 7):
            pe_use_bank(b)

        def ln1_chunk(g):
            sl = g % 2
            sem = f"LG{sl}"
            if g >= 2:
                p.wait("sync", "DV", ln_cdv[g - 2])
                p.wait("sync", "AC", ln_cac[g - 2])
            p.dma("sync", sem, lambda e, g=g, sl=sl: e.dma_start(out=GS(sl),
                                                                in_=x_chunk(g)))
            ldc = p.cnt[sem]
            p.wait("vector", sem, ldc)
            p.wait("vector", "PE", p.cnt["PE"] - 8 if g >= 1 else 0)
            p.op("vector", lambda e, sl=sl: e.tensor_copy(out=TT(sl), in_=GS(sl)))
            ln_cdv[g] = p.cnt["DV"]
            p.wait("scalar", sem, ldc)
            p.wait("scalar", "PE", p.cnt["PE"] - 8 if g >= 1 else 0)
            p.op("scalar", lambda e, sl=sl: e.activation(out=TT(2 + sl), in_=GS(sl),
                                                         func=AF.Square))
            ln_cac[g] = p.cnt["AC"]
            p.wait("tensor", "DV", ln_cdv[g])
            p.wait("tensor", "AC", ln_cac[g])
            for t in range(4):
                p.op("tensor", lambda e, sl=sl, t=t, g=g: e.matmul(
                    out=pb[4 + t][0:1, :], lhsT=onescol_b,
                    rhs=TT(sl)[:, t * 512:(t + 1) * 512],
                    start=(g == 0), stop=(g == NG - 1)), inc="PE")
            for t in range(4):
                p.op("tensor", lambda e, sl=sl, t=t, g=g: e.matmul(
                    out=pb[4 + t][32:33, :], lhsT=onescol_b,
                    rhs=TT(2 + sl)[:, t * 512:(t + 1) * 512],
                    start=(g == 0), stop=(g == NG - 1)), inc="PE")

        for j in range(64):
            sl = j % 2
            sem = f"LW{sl}"
            if j >= 2:
                p.wait("sync", "AC", wpart[j - 2])
            p.dma("sync", sem, lambda e, j=j, sl=sl: e.dma_start(
                out=wbuf[:, sl * 4096:(sl + 1) * 4096].rearrange(
                    "p (s f) -> p s f", f=2048),
                in_=wgu[2 * j:2 * j + 2].rearrange("o p h -> p o h")))
            p.wait("scalar", sem, p.cnt[sem])
            p.op("scalar", lambda e, j=j, sl=sl: e.activation(
                out=wbuf[:, sl * 4096:(sl + 1) * 4096],
                in_=wbuf[:, sl * 4096:(sl + 1) * 4096], func=AF.Abs,
                accum_out=macc[:, j:j + 1]))
            wpart[j] = p.cnt["AC"]
            if j % 4 == 0:
                ln1_chunk(j // 4)
        p.wait("vector", "AC", p.cnt["AC"])
        p.op("vector", lambda e: e.tensor_reduce(out=COL(CTMP1), in_=macc[:, 0:64],
                                                 axis=AX.X, op=ALU.add))
        scalar_final(COL(CTMP1), CSH_GU, CNSH_GU, 2048.0 * 16384.0, banks=(0, 1))
        p.wait("vector", "PE", p.cnt["PE"])
        m_row, rstd_row = ROW(R_M), ROW(R_RSTD)
        for t in range(4):
            p.op("vector", lambda e, t=t: e.tensor_scalar(
                out=m_row[:, t * 512:(t + 1) * 512], in0=pb[4 + t][0:1, :],
                scalar1=1.0 / 2048.0, scalar2=None, op0=ALU.mult))
        for t in range(4):
            p.op("vector", lambda e, t=t: e.tensor_scalar(
                out=rstd_row[:, t * 512:(t + 1) * 512], in0=pb[4 + t][32:33, :],
                scalar1=1.0 / 2048.0, scalar2=None, op0=ALU.mult))
        for b in (4, 5, 6, 7):
            set_bank_free(b)
        p.op("vector", lambda e: e.tensor_tensor(out=tmp[0:1, :], in0=m_row,
                                                 in1=m_row, op=ALU.mult))
        p.op("vector", lambda e: e.tensor_tensor(out=rstd_row, in0=rstd_row,
                                                 in1=tmp[0:1, :], op=ALU.subtract))
        p.wait("scalar", "DV", p.cnt["DV"])
        p.op("scalar", lambda e: e.activation(out=rstd_row, in_=rstd_row,
                                              func=AF.Ln, bias=1e-5))
        p.op("scalar", lambda e: e.activation(out=rstd_row, in_=rstd_row,
                                              func=AF.Exp, scale=-0.5))
        p.wait("vector", "AC", p.cnt["AC"])
        p.wait("tensor", "AC", p.cnt["AC"])

        def ln_stats(src, m_row, rstd_row):
            for b in (4, 5, 6, 7):
                pe_use_bank(b)
            cdv, cac = {}, {}
            ISL = (GS(0), GS(1), WS(0), WS(1))
            ISEM = ("LG0", "LG1", "LW0", "LW1")
            for g in range(NG):
                sl = g % 2
                isl = g % 4
                sem = ISEM[isl]
                if g >= 4:
                    p.wait("sync", "DV", cdv[g - 4])
                    p.wait("sync", "AC", cac[g - 4])
                p.dma("sync", sem, lambda e, g=g, isl=isl: e.dma_start(
                    out=ISL[isl], in_=src(g)))
                ldc = p.cnt[sem]
                p.wait("vector", sem, ldc)
                p.wait("vector", "PE", p.cnt["PE"] - 8 if g >= 1 else 0)
                p.op("vector", lambda e, sl=sl, isl=isl: e.tensor_copy(
                    out=TT(sl), in_=ISL[isl]))
                cdv[g] = p.cnt["DV"]
                p.wait("scalar", sem, ldc)
                p.wait("scalar", "PE", p.cnt["PE"] - 8 if g >= 1 else 0)
                p.op("scalar", lambda e, sl=sl, isl=isl: e.activation(
                    out=TT(2 + sl), in_=ISL[isl], func=AF.Square))
                cac[g] = p.cnt["AC"]
                p.wait("tensor", "DV", cdv[g])
                p.wait("tensor", "AC", cac[g])
                for t in range(4):
                    p.op("tensor", lambda e, sl=sl, t=t, g=g: e.matmul(
                        out=pb[4 + t][0:1, :], lhsT=onescol_b,
                        rhs=TT(sl)[:, t * 512:(t + 1) * 512],
                        start=(g == 0), stop=(g == NG - 1)), inc="PE")
                for t in range(4):
                    p.op("tensor", lambda e, sl=sl, t=t, g=g: e.matmul(
                        out=pb[4 + t][32:33, :], lhsT=onescol_b,
                        rhs=TT(2 + sl)[:, t * 512:(t + 1) * 512],
                        start=(g == 0), stop=(g == NG - 1)), inc="PE")
            p.wait("vector", "PE", p.cnt["PE"])
            for t in range(4):
                p.op("vector", lambda e, t=t: e.tensor_scalar(
                    out=m_row[:, t * 512:(t + 1) * 512], in0=pb[4 + t][0:1, :],
                    scalar1=1.0 / 2048.0, scalar2=None, op0=ALU.mult))
            for t in range(4):
                p.op("vector", lambda e, t=t: e.tensor_scalar(
                    out=rstd_row[:, t * 512:(t + 1) * 512], in0=pb[4 + t][32:33, :],
                    scalar1=1.0 / 2048.0, scalar2=None, op0=ALU.mult))
            for b in (4, 5, 6, 7):
                set_bank_free(b)
            p.op("vector", lambda e: e.tensor_tensor(out=tmp[0:1, :], in0=m_row,
                                                     in1=m_row, op=ALU.mult))
            p.op("vector", lambda e: e.tensor_tensor(out=rstd_row, in0=rstd_row,
                                                     in1=tmp[0:1, :], op=ALU.subtract))
            p.wait("scalar", "DV", p.cnt["DV"])
            p.op("scalar", lambda e: e.activation(out=rstd_row, in_=rstd_row,
                                                  func=AF.Ln, bias=1e-5))
            p.op("scalar", lambda e: e.activation(out=rstd_row, in_=rstd_row,
                                                  func=AF.Exp, scale=-0.5))
            p.wait("vector", "AC", p.cnt["AC"])
            p.wait("tensor", "AC", p.cnt["AC"])

        p.barrier()
        if stage <= 0:
            return _finish(p, nc)

        def ln_maxd(src, m_row, do_wdn=False):
            bcast_row(m_row, BC(0))
            sub_dv = {}
            wdn_dv = {}
            for g in range(NG):
                sl = g % 2
                sem = f"LG{sl}"
                if g >= 2:
                    p.wait("sync", "DV", sub_dv[g - 2])
                p.dma("sync", sem, lambda e, g=g, sl=sl: e.dma_start(out=GS(sl),
                                                                     in_=src(g)))
                if do_wdn:
                    for q4 in range(4):
                        og = g * 4 + q4
                        wsl = og % 2
                        if og >= 2:
                            p.wait("sync", "AC", wdn_dv[og - 2])
                        p.dma("sync", f"LW{wsl}", lambda e, og=og, wsl=wsl:
                              e.dma_start(out=WS(wsl), in_=wdn[og]))
                        p.wait("scalar", f"LW{wsl}", p.cnt[f"LW{wsl}"])
                        p.op("scalar", lambda e, og=og, wsl=wsl: e.activation(
                            out=WS(wsl), in_=WS(wsl), func=AF.Abs,
                            accum_out=macc[:, 256 + og:256 + og + 1]))
                        wdn_dv[og] = p.cnt["AC"]
                p.wait("vector", sem, p.cnt[sem])
                p.wait("vector", "PE", p.cnt["PE"])
                p.op("vector", lambda e, sl=sl: e.tensor_tensor(
                    out=tmp[:], in0=GS(sl), in1=BC(0), op=ALU.subtract))
                sub_dv[g] = p.cnt["DV"]
                p.wait("tensor", "DV", sub_dv[g])
                for i4 in range(4):
                    b = 4 + i4 % 2
                    pe_use_bank(b)
                    for k in range(4):
                        i = i4 * 4 + k
                        p.op("tensor", lambda e, i=i, k=k, b=b: e.transpose(
                            out=pb[b][:, k * 128:(k + 1) * 128],
                            in_=tmp[:, i * 128:(i + 1) * 128], identity=eyef[:]),
                            inc="PE")
                    p.wait("vector", "PE", p.cnt["PE"])
                    p.op("vector", lambda e, i4=i4, g=g, b=b: e.tensor_reduce(
                        out=macc[:].rearrange("p (tb gg) -> p tb gg", gg=16)[
                            :, i4 * 4:(i4 + 1) * 4, g:g + 1],
                        in_=pb[b][:].rearrange("p (s f) -> p s f", f=128),
                        axis=AX.X, op=ALU.max, apply_absolute_value=True))
                    set_bank_free(b)
            p.op("vector", lambda e: e.tensor_reduce(
                out=cols[:, CMAXD:CMAXD + 16].rearrange("p (s o) -> p s o", o=1),
                in_=macc[:, 0:256].rearrange("p (tb gg) -> p tb gg", gg=16),
                axis=AX.X, op=ALU.max))

        def quant_rows(rstd_row, cs_row, phi_row, csh, bslot):
            row_from_cols(cols[:, CMAXD:CMAXD + 16], tmp[0:1, :], bslot)
            p.op("vector", lambda e: e.tensor_tensor(out=tmp[0:1, :], in0=tmp[0:1, :],
                                                     in1=rstd_row, op=ALU.mult))
            p.op("vector", lambda e: e.tensor_scalar(out=tmp[0:1, :], in0=tmp[0:1, :],
                                                     scalar1=1e-5, scalar2=None,
                                                     op0=ALU.max))
            p.wait("scalar", "DV", p.cnt["DV"])
            p.op("scalar", lambda e: e.activation(out=cs_row, in_=tmp[0:1, :],
                                                  func=AF.Ln))
            p.op("scalar", lambda e: e.activation(out=cs_row, in_=cs_row, func=AF.Exp,
                                                  scale=-1.0, bias=LN127))
            p.wait("vector", "AC", p.cnt["AC"])
            p.op("vector", lambda e: e.tensor_tensor(out=cs_row, in0=cs_row,
                                                     in1=rstd_row, op=ALU.mult))
            p.op("vector", lambda e, csh=csh: e.tensor_scalar(
                out=phi_row, in0=tmp[0:1, :], scalar1=cols[0:1, csh:csh + 1],
                scalar2=2.0 / 127.0, op0=ALU.mult, op1=ALU.mult))
            p.wait("scalar", "DV", p.cnt["DV"])
            p.wait("tensor", "DV", p.cnt["DV"])

        ln_maxd(x_chunk, ROW(R_M), do_wdn=True)
        quant_rows(ROW(R_RSTD), ROW(R_CS), ROW(R_PHI), CSH_GU, 0)
        p.barrier()
        if stage <= 1:
            return _finish(p, nc)

        def ln_quant(src, cs_row):
            bcast_row(cs_row, BC(1))
            qdv = {}
            for g in range(NG):
                sl = g % 2
                sem = f"LG{sl}"
                if g >= 2:
                    p.wait("sync", "DV", qdv[g - 2])
                p.dma("sync", sem, lambda e, g=g, sl=sl: e.dma_start(out=GS(sl),
                                                                     in_=src(g)))
                p.wait("vector", sem, p.cnt[sem])
                p.op("vector", lambda e, sl=sl: e.tensor_tensor(
                    out=tmp[:], in0=GS(sl), in1=BC(0), op=ALU.subtract))
                p.op("vector", lambda e: e.tensor_tensor(out=tmp[:], in0=tmp[:],
                                                         in1=BC(1), op=ALU.mult))
                p.op("vector", lambda e: e.tensor_scalar(out=tmp[:], in0=tmp[:],
                                                         scalar1=MAGIC, scalar2=None,
                                                         op0=ALU.add))
                p.op("vector", lambda e, g=g: e.tensor_scalar(
                    out=AR(g), in0=tmp[:], scalar1=MAGIC, scalar2=None,
                    op0=ALU.subtract))
                qdv[g] = p.cnt["DV"]

        ln_quant(x_chunk, ROW(R_CS))
        p.wait("vector", "AC", p.cnt["AC"])
        p.op("vector", lambda e: e.tensor_reduce(out=COL(CTMP1),
                                                 in_=macc[:, 256:320],
                                                 axis=AX.X, op=ALU.add))
        scalar_final(COL(CTMP1), CSH_DN, CNSH_DN, 8192.0 * 2048.0)
        p.barrier()
        if stage <= 2:
            return _finish(p, nc)

        def gu_era(first_use, phi_row):
            bcast_row(phi_row, BC(0))
            tern_dv, mm_pe, g_dv = {}, {}, {}
            pend = []
            tr_pe = {}

            def flush_pend():
                while pend:
                    (pj, pt, pgsl, pdv) = pend.pop(0)
                    btr = 4 + pt % 2
                    pe_use_bank(btr)
                    p.wait("tensor", "DV", pdv)
                    for k in range(4):
                        p.op("tensor", lambda e, pt=pt, k=k, btr=btr, pgsl=pgsl:
                             e.transpose(
                                 out=pb[btr][:, k * 128:(k + 1) * 128],
                                 in_=GS(pgsl)[:, pt * 512 + k * 128:
                                              pt * 512 + (k + 1) * 128],
                                 identity=eyef[:]), inc="PE")
                    tr_pe[pj] = p.cnt["PE"]
                    p.wait("vector", "PE", p.cnt["PE"])
                    p.op("vector", lambda e, pj=pj, pt=pt, btr=btr: e.tensor_reduce(
                        out=macc[:].rearrange("p (tb jj) -> p tb jj", jj=64)[
                            :, pt * 4:(pt + 1) * 4, pj:pj + 1],
                        in_=pb[btr][:].rearrange("p (s f) -> p s f", f=128),
                        axis=AX.X, op=ALU.max, apply_absolute_value=True))
                    set_bank_free(btr)

            def emit_tern(jj):
                wa, wb_ = (jj % 2) * 2, (jj % 2) * 2 + 1
                ta, tb_ = (jj % 2) * 2, (jj % 2) * 2 + 1
                if jj >= 2:
                    p.wait("sync", "DV", tern_dv[jj - 2])
                p.dma("sync", f"LW{wa}", lambda e, jj=jj, s=wa: e.dma_start(
                    out=WS(s), in_=wgu[jj]))
                p.dma("sync", f"LW{wb_}", lambda e, jj=jj, s=wb_: e.dma_start(
                    out=WS(s), in_=wgu[jj + 64]))
                p.wait("vector", f"LW{wa}", p.cnt[f"LW{wa}"])
                p.wait("vector", f"LW{wb_}", p.cnt[f"LW{wb_}"])
                if jj >= 2:
                    p.wait("vector", "PE", mm_pe[jj - 2])
                    p.wait("vector", f"TT{ta}", p.cnt[f"TT{ta}"])
                    p.wait("vector", f"TT{tb_}", p.cnt[f"TT{tb_}"])
                for (ws, tt) in ((wa, ta), (wb_, tb_)):
                    p.op("vector", lambda e, ws=ws: e.tensor_scalar(
                        out=bcst[:, 2048:4096], in0=WS(ws), scalar1=COL(CNSH_GU),
                        scalar2=None, op0=ALU.is_lt))
                    p.op("vector", lambda e, ws=ws, tt=tt: e.scalar_tensor_tensor(
                        out=TT(tt), in0=WS(ws), scalar=COL(CSH_GU),
                        in1=bcst[:, 2048:4096], op0=ALU.is_gt, op1=ALU.subtract))
                tern_dv[jj] = p.cnt["DV"]
                p.wait("scalar", "DV", tern_dv[jj])
                p.dma("scalar", f"TT{ta}", lambda e, jj=jj, tt=ta: e.dma_start(
                    out=terngu[jj], in_=TT(tt)))
                p.dma("scalar", f"TT{tb_}", lambda e, jj=jj, tt=tb_: e.dma_start(
                    out=terngu[jj + 64], in_=TT(tt)))

            if first_use:
                emit_tern(0)
            for j in range(NPAIR):
                wa, wb_ = (j % 2) * 2, (j % 2) * 2 + 1
                ta, tb_ = (j % 2) * 2, (j % 2) * 2 + 1
                if first_use:
                    if j + 1 < NPAIR:
                        emit_tern(j + 1)
                    p.wait("tensor", "DV", tern_dv[j])
                else:
                    if j >= 2:
                        p.wait("sync", "PE", mm_pe[j - 2])
                    p.dma("sync", f"LT{ta}", lambda e, j=j, s=ta: e.dma_start(
                        out=TT(s), in_=terngu[j]))
                    p.dma("sync", f"LT{tb_}", lambda e, j=j, s=tb_: e.dma_start(
                        out=TT(s), in_=terngu[j + 64]))
                    p.wait("tensor", f"LT{ta}", p.cnt[f"LT{ta}"])
                    p.wait("tensor", f"LT{tb_}", p.cnt[f"LT{tb_}"])
                gsl = j % 2
                for t in range(4):
                    ba, bb = t % 2, 2 + t % 2
                    pe_use_bank(ba)
                    for g in range(NG):
                        p.op("tensor", lambda e, g=g, t=t, tt=ta, ba=ba: e.matmul(
                            out=pb[ba][:], lhsT=TT(tt)[:, g * 128:(g + 1) * 128],
                            rhs=AR(g)[:, t * 512:(t + 1) * 512],
                            start=(g == 0), stop=(g == NG - 1)), inc="PE")
                    pe_a = p.cnt["PE"]
                    pe_use_bank(bb)
                    for g in range(NG):
                        p.op("tensor", lambda e, g=g, t=t, tt=tb_, bb=bb: e.matmul(
                            out=pb[bb][:], lhsT=TT(tt)[:, g * 128:(g + 1) * 128],
                            rhs=AR(g)[:, t * 512:(t + 1) * 512],
                            start=(g == 0), stop=(g == NG - 1)), inc="PE")
                    pe_b = p.cnt["PE"]
                    flush_pend()
                    if t == 0 and j >= 2:
                        p.wait("vector", f"TG{gsl}", p.cnt[f"TG{gsl}"])
                        p.wait("vector", "PE", tr_pe.get(j - 2, 0))
                    p.wait("vector", "PE", pe_a)
                    p.op("vector", lambda e, t=t, ba=ba: e.tensor_tensor(
                        out=tmp[:, t * 512:(t + 1) * 512], in0=pb[ba][:],
                        in1=BC(0)[:, t * 512:(t + 1) * 512], op=ALU.mult))
                    set_bank_free(ba)
                    p.wait("scalar", "DV", p.cnt["DV"])
                    p.op("scalar", lambda e, t=t: e.activation(
                        out=tmp[:, t * 512:(t + 1) * 512],
                        in_=tmp[:, t * 512:(t + 1) * 512], func=AF.Silu))
                    p.wait("vector", "PE", pe_b)
                    p.wait("vector", "AC", p.cnt["AC"])
                    p.op("vector", lambda e, t=t, bb=bb, gsl=gsl: e.tensor_tensor(
                        out=GS(gsl)[:, t * 512:(t + 1) * 512],
                        in0=tmp[:, t * 512:(t + 1) * 512], in1=pb[bb][:], op=ALU.mult))
                    set_bank_free(bb)
                    g_dv[(j, t)] = p.cnt["DV"]
                    pend.append((j, t, gsl, p.cnt["DV"]))
                mm_pe[j] = p.cnt["PE"]
                p.wait("scalar", "DV", g_dv[(j, 3)])
                for t in range(4):
                    p.dma("scalar", f"TG{gsl}", lambda e, j=j, t=t, gsl=gsl:
                          e.dma_start(out=g1d[t][:, j * 512:(j + 1) * 512],
                                      in_=GS(gsl)[:, t * 512:(t + 1) * 512]))
            flush_pend()

        gu_era(True, ROW(R_PHI))
        p.barrier()
        if stage <= 3:
            return _finish(p, nc)

        def g_rows_a(phi_row, q2_row, bslot):
            p.op("vector", lambda e: e.tensor_reduce(
                out=cols[:, CMAXD:CMAXD + 16].rearrange("p (s o) -> p s o", o=1),
                in_=macc[:].rearrange("p (tb jj) -> p tb jj", jj=64),
                axis=AX.X, op=ALU.max))
            row_from_cols(cols[:, CMAXD:CMAXD + 16], tmp[0:1, :], bslot)
            p.op("vector", lambda e: e.tensor_tensor(out=tmp[0:1, :], in0=tmp[0:1, :],
                                                     in1=phi_row, op=ALU.mult))
            p.op("vector", lambda e: e.tensor_scalar(out=ROW(R_CS), in0=tmp[0:1, :],
                                                     scalar1=1e-5, scalar2=None,
                                                     op0=ALU.max))
            p.wait("scalar", "DV", p.cnt["DV"])
            p.op("scalar", lambda e: e.activation(out=q2_row, in_=ROW(R_CS),
                                                  func=AF.Ln))
            p.op("scalar", lambda e: e.activation(out=q2_row, in_=q2_row, func=AF.Exp,
                                                  scale=-1.0, bias=LN127))
            p.wait("vector", "AC", p.cnt["AC"])
            p.op("vector", lambda e: e.tensor_tensor(out=q2_row, in0=q2_row,
                                                     in1=phi_row, op=ALU.mult))

        def f2_calc(f2_row):
            p.op("vector", lambda e: e.tensor_scalar(
                out=f2_row, in0=ROW(R_CS), scalar1=cols[0:1, CSH_DN:CSH_DN + 1],
                scalar2=2.0 / 127.0, op0=ALU.mult, op1=ALU.mult))
            p.wait("tensor", "DV", p.cnt["DV"])
            p.wait("scalar", "DV", p.cnt["DV"])

        g_rows_a(ROW(R_PHI), ROW(R_Q2), 1)
        f2_calc(ROW(R_F2))
        p.barrier()
        if stage <= 4:
            return _finish(p, nc)

        RS4 = lambda l: wbuf[:, 4096 + l * 512:4608 + l * 512]
        RSEMS = ("LR0", "LR1", "LS0", "LS1")
        def dn_era(first_use, q2_row, f2_row, res_src, dst4):
            bcast_row(q2_row, BC(0))
            bcast_row(f2_row, BC(1))
            gq_dv = {}
            t4_pe = {i: 0 for i in range(4)}
            t4_ld = {}
            tern_pe = {0: 0, 1: 0}
            wdnt_dv = {}
            res4_dv = {l: 0 for l in range(4)}
            res4_ld = {l: 0 for l in range(4)}
            for ti in range(4):
                first00 = first_use and ti == 0

                def emit_res(hq, ti=ti):
                    for l in range(4):
                        ci = hq * 4 + l
                        p.wait("sync", "DV", res4_dv[l])
                        p.dma("sync", RSEMS[l], lambda e, ci=ci, ti=ti, l=l:
                              e.dma_start(out=RS4(l), in_=res_src(ci, ti)))
                        res4_ld[l] = p.cnt[RSEMS[l]]

                def emit_t4(hq, b, ti=ti):
                    sl = b % 4
                    lsem = f"LT{sl}"
                    p.wait("sync", "PE", t4_pe[sl])
                    p.wait("sync", lsem, p.cnt[lsem])
                    if first_use:
                        p.wait("sync", "TT0", p.cnt["TT0"])
                        p.wait("sync", "TT1", p.cnt["TT1"])
                    p.dma("sync", lsem, lambda e, hq=hq, b=b, sl=sl: e.dma_start(
                        out=TT(sl), in_=terndn[hq][:, b * 2048:(b + 1) * 2048]))
                    t4_ld[(hq, b)] = p.cnt[lsem]

                def emit_mm4(hq, b, ti=ti):
                    sl = b % 4
                    bk = ((ti * 4 + hq) % 2) * 4
                    p.wait("tensor", f"LT{sl}", t4_ld[(hq, b)])
                    p.wait("tensor", "DV", gq_dv[b])
                    if b == 0:
                        for l in range(4):
                            pe_use_bank(bk + l)
                    for ko in range(4):
                        og = b * 4 + ko
                        for l in range(4):
                            p.op("tensor", lambda e, l=l, og=og, ko=ko, sl=sl,
                                 bb=bk: e.matmul(
                                     out=pb[bb + l][:],
                                     lhsT=TT(sl)[:, ko * 512 + l * 128:
                                                 ko * 512 + (l + 1) * 128],
                                     rhs=AR512(og),
                                     start=(og == 0), stop=(og == NOG - 1)),
                                 inc="PE")
                    t4_pe[sl] = p.cnt["PE"]

                def epilogue(hq, ti=ti):
                    pe_done = p.cnt["PE"]
                    bk = ((ti * 4 + hq) % 2) * 4
                    tsem = f"TS{hq % 2}"
                    eb = (hq % 2) * 2048
                    for l in range(4):
                        p.wait("vector", "PE", pe_done)
                        p.wait("vector", RSEMS[l], res4_ld[l])
                        if l == 0:
                            p.wait("vector", tsem, p.cnt[tsem])
                        p.op("vector", lambda e, l=l, ti=ti, b=bk + l: e.tensor_tensor(
                            out=tmp[:, 0:512], in0=pb[b][:],
                            in1=BC(1)[:, ti * 512:(ti + 1) * 512], op=ALU.mult))
                        set_bank_free(bk + l)
                        p.op("vector", lambda e, l=l, eb=eb: e.scalar_tensor_tensor(
                            out=rows2[:, eb + l * 512:eb + (l + 1) * 512],
                            in0=tmp[:, 0:512], scalar=0.5, in1=RS4(l),
                            op0=ALU.mult, op1=ALU.add))
                        res4_dv[l] = p.cnt["DV"]
                    p.wait("scalar", "DV", p.cnt["DV"])
                    p.wait("scalar", tsem, p.cnt[tsem])
                    p.dma("scalar", tsem, lambda e, hq=hq, ti=ti, eb=eb:
                          e.dma_start(out=dst4(hq, ti), in_=rows2[:, eb:eb + 2048]))

                p.wait("vector", "PE", p.cnt["PE"])
                for gb in range(NG):
                    sl = gb % 2
                    sem = f"LG{sl}"
                    if gb >= 2:
                        p.wait("sync", "DV", gq_dv[gb - 2])
                    p.dma("sync", sem, lambda e, gb=gb, sl=sl, ti=ti: e.dma_start(
                        out=GS(sl), in_=g1d[ti][:, gb * 2048:(gb + 1) * 2048]))
                    if not first00:
                        emit_t4(0, gb)
                        if gb == 13:
                            emit_res(0)
                    p.wait("vector", sem, p.cnt[sem])
                    for so in range(4):
                        p.op("vector", lambda e, sl=sl, ti=ti, so=so: e.tensor_tensor(
                            out=tmp[:, so * 512:(so + 1) * 512],
                            in0=GS(sl)[:, so * 512:(so + 1) * 512],
                            in1=BC(0)[:, ti * 512:(ti + 1) * 512], op=ALU.mult))
                    p.op("vector", lambda e: e.tensor_scalar(
                        out=tmp[:], in0=tmp[:], scalar1=MAGIC, scalar2=None,
                        op0=ALU.add))
                    p.op("vector", lambda e, gb=gb: e.tensor_scalar(
                        out=AR(gb), in0=tmp[:], scalar1=MAGIC, scalar2=None,
                        op0=ALU.subtract))
                    gq_dv[gb] = p.cnt["DV"]
                    if not first00:
                        emit_mm4(0, gb)
                if not first00:
                    epilogue(0)

                if first00:
                    emit_res(0)
                    for og in range(NOG):
                        tsl = og % 2
                        wsem = f"LW{tsl}"
                        if og >= 2:
                            p.wait("sync", "DV", wdnt_dv[og - 2])
                        p.dma("sync", wsem, lambda e, og=og, sl=og % 2:
                              e.dma_start(out=WS(sl), in_=wdn[og]))
                        p.wait("vector", wsem, p.cnt[wsem])
                        p.wait("vector", "PE", tern_pe[tsl])
                        p.wait("vector", f"TT{tsl}", p.cnt[f"TT{tsl}"])
                        p.op("vector", lambda e, sl=og % 2: e.tensor_scalar(
                            out=tmp[:], in0=WS(sl), scalar1=COL(CNSH_DN),
                            scalar2=None, op0=ALU.is_lt))
                        p.op("vector", lambda e, sl=og % 2, tsl=tsl:
                             e.scalar_tensor_tensor(
                                 out=TT(tsl), in0=WS(sl), scalar=COL(CSH_DN),
                                 in1=tmp[:], op0=ALU.is_gt, op1=ALU.subtract))
                        wdnt_dv[og] = p.cnt["DV"]
                        p.wait("scalar", "DV", p.cnt["DV"])
                        for q4 in range(4):
                            p.dma("scalar", f"TT{tsl}",
                                  lambda e, og=og, tsl=tsl, q4=q4: e.dma_start(
                                      out=terndn[q4][:, og * 512:(og + 1) * 512],
                                      in_=TT(tsl)[:, q4 * 512:(q4 + 1) * 512]))
                        p.wait("tensor", "DV", wdnt_dv[og])
                        p.wait("tensor", "DV", gq_dv[og // 4])
                        if og == 0:
                            for l in range(4):
                                pe_use_bank(l)
                        for l in range(4):
                            p.op("tensor", lambda e, l=l, og=og, tsl=tsl: e.matmul(
                                out=pb[l][:],
                                lhsT=TT(tsl)[:, l * 128:(l + 1) * 128],
                                rhs=AR512(og),
                                start=(og == 0), stop=(og == NOG - 1)), inc="PE")
                        tern_pe[tsl] = p.cnt["PE"]
                    for i in range(4):
                        t4_pe[i] = p.cnt["PE"]
                    epilogue(0)
                for hq in range(1, 4):
                    emit_t4(hq, 0)
                    emit_t4(hq, 1)
                    emit_t4(hq, 2)
                    emit_t4(hq, 3)
                    for b in range(NG):
                        if b + 4 < NG:
                            pass
                        emit_mm4(hq, b)
                        if b + 4 < NG:
                            emit_t4(hq, b + 4)
                        if b == 13:
                            emit_res(hq)
                    epilogue(hq)

        res_src1 = lambda ci, ti: x_cT[ci * 128:(ci + 1) * 128,
                                       ti * 512:(ti + 1) * 512]
        dst1 = lambda hq, ti: x1d[hq * 4:(hq + 1) * 4, :,
                                  ti * 512:(ti + 1) * 512].rearrange(
                                      "c p s -> p c s")
        dn_era(True, ROW(R_Q2), ROW(R_F2), res_src1, dst1)
        p.barrier()
        if stage <= 5:
            return _finish(p, nc)

        x1_chunk = lambda g: x1d[g]
        ln_stats(x1_chunk, ROW(R_M), ROW(R_RSTD))
        p.barrier()
        if stage <= 6:
            return _finish(p, nc)

        bcast_row(ROW(R_M), BC(0))
        bcast_row(ROW(R_RSTD), BC(1))
        zdv = {}
        for g in range(NG):
            sl = g % 2
            sem = f"LG{sl}"
            if g >= 2:
                p.wait("sync", "DV", zdv[g - 2])
            p.dma("sync", sem, lambda e, g=g, sl=sl: e.dma_start(out=GS(sl),
                                                                 in_=x1d[g]))
            p.wait("vector", sem, p.cnt[sem])
            p.op("vector", lambda e, sl=sl: e.tensor_tensor(
                out=tmp[:], in0=GS(sl), in1=BC(0), op=ALU.subtract))
            p.op("vector", lambda e, g=g: e.tensor_tensor(
                out=AR(g), in0=tmp[:], in1=BC(1), op=ALU.mult))
            zdv[g] = p.cnt["DV"]
            p.wait("gpsimd", "DV", zdv[g])
            p.dma("gpsimd", "TZ", lambda e, g=g: e.dma_start(out=zd[g], in_=AR(g)))
        p.barrier()
        if stage <= 7:
            return _finish(p, nc)

        ev_sg = {}

        def proj_era(tm_cbase, wmat, evac):
            mdv, mix_pl = {}, {}
            for g in range(NG):
                zsl = 2 + g % 2
                zsem = f"LT{zsl}"
                scr = tmp if g % 2 == 0 else rows[:, 2048:4096]
                if g >= 2:
                    p.wait("sync", "DV", mdv[g - 2])
                    p.wait("sync", "PL", mix_pl.get(g - 2, 0))
                p.dma("sync", zsem, lambda e, g=g, zsl=zsl: e.dma_start(out=TT(zsl),
                                                                        in_=zd[g]))
                p.wait("gpsimd", zsem, p.cnt[zsem])
                p.wait("gpsimd", "DV", mdv.get(g - 2, 0))
                p.op("gpsimd", lambda e, zsl=zsl, scr=scr: e.tensor_tensor(
                    out=scr[:, 1:2048], in0=TT(zsl)[:, 1:2048],
                    in1=TT(zsl)[:, 0:2047], op=ALU.subtract))
                mix_pl[g] = p.cnt["PL"]
                p.wait("vector", zsem, p.cnt[zsem])
                p.wait("vector", "PL", mix_pl[g])
                p.op("vector", lambda e, g=g, zsl=zsl, cb=tm_cbase, scr=scr:
                     e.scalar_tensor_tensor(
                         out=AR(g)[:, 1:2048], in0=scr[:, 1:2048],
                         scalar=cols[:, cb + g:cb + g + 1],
                         in1=TT(zsl)[:, 0:2047],
                         op0=ALU.mult, op1=ALU.add))
                p.op("vector", lambda e, zsl=zsl, scr=scr: e.tensor_copy(
                    out=scr[:, 0:1], in_=TT(zsl)[:, 0:1]))
                p.op("vector", lambda e, g=g, cb=tm_cbase, scr=scr: e.tensor_scalar(
                    out=AR(g)[:, 0:1], in0=scr[:, 0:1],
                    scalar1=cols[:, cb + g:cb + g + 1], scalar2=None, op0=ALU.mult))
                mdv[g] = p.cnt["DV"]
            p.wait("tensor", "DV", p.cnt["DV"])
            p.wait("tensor", "PL", p.cnt["PL"])
            mixpl_total = p.cnt["PL"]
            wconv, wmm_pe = {}, {}
            NSL = 2 if evac == "r" else 4

            def emit_conv(dc):
                wsl = dc % 2
                tsl = dc % NSL
                wsem = f"LW{wsl}"
                if dc >= 2:
                    p.wait("sync", "DV", wconv[dc - 2])
                p.dma("sync", wsem, lambda e, dc=dc, wsl=wsl: e.dma_start(
                    out=WS(wsl), in_=wmat[dc]))
                p.wait("vector", wsem, p.cnt[wsem])
                p.wait("vector", "PL", mixpl_total)
                if dc >= NSL:
                    p.wait("vector", "PE", wmm_pe[dc - NSL])
                p.op("vector", lambda e, tsl=tsl: e.tensor_copy(out=TT(tsl),
                                                                in_=WS(wsl)))
                wconv[dc] = p.cnt["DV"]

            ST_A = (WS(3), rows[:, 0:2048])
            ST_V = (GS(1), rows[:, 2048:4096])
            st_av, st_bv = GS(0), WS(2)
            t_ew = macc[:, 0:1024]
            t_nacc2 = bcst[:, 0:2048]
            t_den2 = bcst[:, 2048:4096]
            t_dacc2 = rows2[:, 0:2048]
            t_num2 = rows2[:, 2048:4096]
            t_rec2 = t_dacc2
            t_wkv2 = t_nacc2
            scan_it_dv = {}
            scan_nacc_dv = {}
            scan_dacc_dv = {}
            scan_num_dv = {}
            scan_ac = {}
            scan_pl_bv = {}
            scan_pl_av = {}
            rq_tr = {}

            mix_pl0 = p.cnt["PL"]

            def emit_scan_loads(dg):
                par = dg % 2
                sa, sv = (f"LS{par * 2}", f"LS{par * 2 + 1}")
                p.wait("sync", "DV", scan_it_dv.get(dg - 2, 0))
                p.wait("sync", "AC", scan_ac.get(dg - 2, 0))
                p.wait("sync", "PL", scan_pl_bv.get(dg - 2, mix_pl0))
                p.wait("sync", sa, p.cnt[sa])
                p.dma("sync", sa, lambda e, dg=dg, par=par: e.dma_start(
                    out=ST_A[par], in_=kvsp[dg]))
                p.wait("sync", sv, p.cnt[sv])
                p.dma("sync", sv, lambda e, dg=dg, par=par: e.dma_start(
                    out=ST_V[par], in_=vsp[dg]))
                p.wait("gpsimd", sa, p.cnt[sa])
                p.wait("gpsimd", sv, p.cnt[sv])
                p.wait("gpsimd", "DV", scan_nacc_dv.get(dg - 1, 0))
                p.op("gpsimd", lambda e, par=par: e.tensor_tensor(
                    out=st_av, in0=ST_A[par], in1=ST_V[par], op=ALU.mult))
                scan_pl_av[dg] = p.cnt["PL"]

            def emit_scan(dg):
                par = dg % 2
                sa, sv = (f"LS{par * 2}", f"LS{par * 2 + 1}")
                st_a, st_v = ST_A[par], ST_V[par]
                st_b = st_a
                p.wait("scalar", "DV", scan_dacc_dv.get(dg - 1, 0))
                p.op("scalar", lambda e, dg=dg: e.activation(
                    out=t_ew, in_=tmp[:, 0:1024], func=AF.Identity, scale=0.0,
                    bias=cols[:, CEW + dg:CEW + dg + 1]))
                ew_ac = p.cnt["AC"]
                p.wait("vector", "AC", ew_ac)
                p.wait("vector", "PL", scan_pl_av[dg])
                p.op("vector", lambda e: e.tensor_tensor_scan(
                    out=t_nacc2[:, 0:1024], data0=t_ew, data1=st_av[:, 0:1024],
                    initial=0.0, op0=ALU.mult, op1=ALU.add))
                p.op("vector", lambda e: e.tensor_tensor_scan(
                    out=t_nacc2[:, 1024:2048], data0=t_ew, data1=st_av[:, 1024:2048],
                    initial=t_nacc2[:, 1023:1024], op0=ALU.mult, op1=ALU.add))
                scan_nacc_dv[dg] = p.cnt["DV"]
                p.wait("vector", sa, p.cnt[sa])
                p.op("vector", lambda e, st_a=st_a: e.tensor_tensor_scan(
                    out=t_dacc2[:, 0:1024], data0=t_ew, data1=st_a[:, 0:1024],
                    initial=0.0, op0=ALU.mult, op1=ALU.add))
                p.op("vector", lambda e, st_a=st_a: e.tensor_tensor_scan(
                    out=t_dacc2[:, 1024:2048], data0=t_ew, data1=st_a[:, 1024:2048],
                    initial=t_dacc2[:, 1023:1024], op0=ALU.mult, op1=ALU.add))
                scan_dacc_dv[dg] = p.cnt["DV"]
                p.wait("scalar", "DV", scan_dacc_dv[dg])
                p.wait("scalar", "PL", scan_pl_av[dg])
                p.op("scalar", lambda e, dg=dg, st_b=st_b, st_a=st_a: e.activation(
                    out=st_b, in_=st_a, func=AF.Identity,
                    scale=cols[:, 112 + dg:112 + dg + 1]))
                scan_ac[dg] = p.cnt["AC"]
                p.wait("gpsimd", "AC", scan_ac[dg])
                p.wait("gpsimd", "DV", scan_num_dv.get(dg - 1, 0))
                p.op("gpsimd", lambda e, st_b=st_b, st_v=st_v: e.tensor_tensor(
                    out=st_bv, in0=st_b, in1=st_v, op=ALU.mult))
                scan_pl_bv[dg] = p.cnt["PL"]
                p.wait("vector", "AC", scan_ac[dg])
                p.op("vector", lambda e, st_b=st_b: e.tensor_tensor(
                    out=t_den2[:, 1:2048], in0=t_dacc2[:, 0:2047],
                    in1=st_b[:, 1:2048], op=ALU.add))
                p.op("vector", lambda e, st_b=st_b: e.tensor_copy(
                    out=t_den2[:, 0:1], in_=st_b[:, 0:1]))
                den_dv = p.cnt["DV"]
                p.wait("vector", "PL", scan_pl_bv[dg])
                p.op("vector", lambda e: e.tensor_tensor(
                    out=t_num2[:, 1:2048], in0=t_nacc2[:, 0:2047],
                    in1=st_bv[:, 1:2048], op=ALU.add))
                p.op("vector", lambda e: e.tensor_copy(
                    out=t_num2[:, 0:1], in_=st_bv[:, 0:1]))
                scan_num_dv[dg] = p.cnt["DV"]
                p.wait("scalar", "DV", den_dv)
                p.op("scalar", lambda e: e.activation(out=t_rec2, in_=t_den2,
                                                      func=AF.Ln))
                p.op("scalar", lambda e: e.activation(out=t_rec2, in_=t_rec2,
                                                      func=AF.Exp, scale=-1.0))
                rec_ac = p.cnt["AC"]
                p.wait("vector", "AC", rec_ac)
                p.op("vector", lambda e: e.tensor_tensor(
                    out=t_wkv2, in0=t_num2, in1=t_rec2, op=ALU.mult))
                rqs = 2 + par
                if dg < 2:
                    p.wait("vector", f"LT{rqs}", p.cnt[f"LT{rqs}"])
                p.wait("vector", f"TR{par}", rq_tr.get(dg - 2, 0))
                p.op("vector", lambda e, rqs=rqs: e.tensor_tensor(
                    out=TT(rqs), in0=t_wkv2, in1=tmp[:], op=ALU.mult))
                scan_it_dv[dg] = p.cnt["DV"]
                p.wait("scalar", "DV", scan_it_dv[dg])
                p.wait("scalar", f"TR{par}", p.cnt[f"TR{par}"])
                p.dma("scalar", f"TR{par}", lambda e, dg=dg, rqs=rqs: e.dma_start(
                    out=zd[dg], in_=TT(rqs)))
                rq_tr[dg] = p.cnt[f"TR{par}"]

            for i0 in range(NSL - 1):
                emit_conv(i0)
            for dc in range(NG):
                wsl = dc % NSL
                if dc + NSL - 1 < NG:
                    emit_conv(dc + NSL - 1)
                if evac == "r":
                    if dc >= 1:
                        emit_scan(dc - 1)
                    emit_scan_loads(dc)
                p.wait("tensor", "DV", wconv[dc])
                for t in range(4):
                    b = (dc % 2) * 4 + t
                    pe_use_bank(b)
                    for g in range(NG):
                        p.op("tensor", lambda e, g=g, t=t, wsl=wsl, b=b: e.matmul(
                            out=pb[b][:], lhsT=TT(wsl)[:, g * 128:(g + 1) * 128],
                            rhs=AR(g)[:, t * 512:(t + 1) * 512],
                            start=(g == 0), stop=(g == NG - 1)), inc="PE")
                    peg = p.cnt["PE"]
                    if evac == "v":
                        stg = tmp if dc % 2 == 0 else rows2
                        tsem = f"TS{dc % 2}"
                        p.wait("vector", "PE", peg)
                        if t == 0:
                            p.wait("vector", tsem, p.cnt[tsem])
                        p.op("vector", lambda e, b=b, t=t, stg=stg: e.tensor_copy(
                            out=stg[:, t * 512:(t + 1) * 512], in_=pb[b][:]))
                        set_bank_free(b)
                        if t == 3:
                            p.wait("scalar", "DV", p.cnt["DV"])
                            p.wait("scalar", tsem, p.cnt[tsem])
                            p.dma("scalar", tsem, lambda e, dc=dc, stg=stg:
                                  e.dma_start(out=vsp[dc], in_=stg[:, 0:2048]))
                    elif evac == "k":
                        stg = tmp if dc % 2 == 0 else rows2
                        ksem = f"TS{dc % 2}"
                        p.wait("scalar", "PE", peg)
                        if t == 0:
                            p.wait("scalar", ksem, p.cnt[ksem])
                        p.op("scalar", lambda e, b=b, t=t, stg=stg: e.activation(
                            out=stg[:, t * 512:(t + 1) * 512], in_=pb[b][:],
                            func=AF.Exp))
                        set_bank_free(b, "AC")
                        if t == 3:
                            p.wait("scalar", "AC", p.cnt["AC"])
                            p.dma("scalar", ksem, lambda e, dc=dc, stg=stg:
                                  e.dma_start(out=kvsp[dc], in_=stg[:, 0:2048]))
                    else:
                        p.wait("scalar", "PE", peg)
                        p.wait("scalar", "DV", scan_it_dv.get(dc - 1, 0))
                        p.op("scalar", lambda e, b=b, t=t: e.activation(
                            out=tmp[:, t * 512:(t + 1) * 512], in_=pb[b][:],
                            func=AF.Sigmoid))
                        set_bank_free(b, "AC")
                wmm_pe[dc] = p.cnt["PE"]
            if evac == "r":
                emit_scan(NG - 1)
            p.barrier()

        proj_era(CTMV, wv, "v")
        if stage <= 8:
            return _finish(p, nc)
        proj_era(CTMK, wk, "k")
        if stage <= 9:
            return _finish(p, nc)
        proj_era(CTMR, wr, "r")
        if stage <= 10:
            return _finish(p, nc)

        rql = {}
        for dg in range(NG):
            sem = f"LT{dg % 4}"
            p.wait("sync", sem, p.cnt[sem])
            p.dma("sync", sem, lambda e, dg=dg: e.dma_start(out=AR(dg), in_=zd[dg]))
            rql[dg] = (sem, p.cnt[sem])
        for dg in range(NG):
            p.wait("tensor", rql[dg][0], rql[dg][1])
        if stage <= 11:
            return _finish(p, nc)

        wo_conv, wo_pe, x1dv = {}, {}, {}

        def emit_woconv(hc):
            wsl = hc % 2
            tsl = hc % 4
            wsem = f"LW{wsl}"
            if hc >= 2:
                p.wait("sync", "DV", wo_conv[hc - 2])
            p.dma("sync", wsem, lambda e, hc=hc, wsl=wsl: e.dma_start(out=WS(wsl),
                                                                      in_=wo[hc]))
            p.wait("vector", wsem, p.cnt[wsem])
            if hc >= 4:
                p.wait("vector", "PE", wo_pe[hc - 4])
            p.op("vector", lambda e, tsl=tsl: e.tensor_copy(out=TT(tsl), in_=WS(wsl)))
            wo_conv[hc] = p.cnt["DV"]

        emit_woconv(0)
        emit_woconv(1)
        emit_woconv(2)
        for hc in range(NG):
            wsl = hc % 4
            if hc + 3 < NG:
                emit_woconv(hc + 3)
            p.wait("tensor", "DV", wo_conv[hc])
            for t in range(4):
                b = (hc % 2) * 4 + t
                pe_use_bank(b)
                for g in range(NG):
                    p.op("tensor", lambda e, g=g, t=t, wsl=wsl, b=b: e.matmul(
                        out=pb[b][:], lhsT=TT(wsl)[:, g * 128:(g + 1) * 128],
                        rhs=AR(g)[:, t * 512:(t + 1) * 512],
                        start=(g == 0), stop=(g == NG - 1)), inc="PE")
                peg = p.cnt["PE"]
                rsl = t % 2
                p.wait("sync", "DV", x1dv.get(rsl, 0))
                p.dma("sync", f"LR{rsl}", lambda e, hc=hc, t=t, rsl=rsl: e.dma_start(
                    out=RS(rsl), in_=x1d[hc][:, t * 512:(t + 1) * 512]))
                ldx = p.cnt[f"LR{rsl}"]
                stg = tmp if hc % 2 == 0 else rows2
                tsem = f"TS{hc % 2}"
                p.wait("vector", "PE", peg)
                p.wait("vector", f"LR{rsl}", ldx)
                if t == 0:
                    p.wait("vector", tsem, p.cnt[tsem])
                p.op("vector", lambda e, t=t, b=b, rsl=rsl, stg=stg: e.tensor_tensor(
                    out=stg[:, t * 512:(t + 1) * 512], in0=pb[b][:], in1=RS(rsl),
                    op=ALU.add))
                set_bank_free(b)
                x1dv[rsl] = p.cnt["DV"]
                if t == 3:
                    p.wait("scalar", "DV", p.cnt["DV"])
                    p.wait("scalar", tsem, p.cnt[tsem])
                    p.dma("scalar", tsem, lambda e, hc=hc, stg=stg: e.dma_start(
                        out=x2d[hc], in_=stg[:, 0:2048]))
            wo_pe[hc] = p.cnt["PE"]
        p.barrier()
        if stage <= 12:
            return _finish(p, nc)

        x2_chunk = lambda g: x2d[g]
        ln_stats(x2_chunk, ROW(R_M), ROW(R_RSTD))
        p.barrier()
        ln_maxd(x2_chunk, ROW(R_M))
        quant_rows(ROW(R_RSTD), ROW(R_CS), ROW(R_PHI), CSH_GU, 2)
        p.barrier()
        ln_quant(x2_chunk, ROW(R_CS))
        p.barrier()
        if stage <= 13:
            return _finish(p, nc)

        gu_era(False, ROW(R_PHI))
        p.barrier()
        if stage <= 14:
            return _finish(p, nc)

        g_rows_a(ROW(R_PHI), ROW(R_Q2), 3)
        f2_calc(ROW(R_F2))
        p.barrier()
        if stage <= 15:
            return _finish(p, nc)

        res_src2 = lambda ci, ti: x2d[ci][:, ti * 512:(ti + 1) * 512]
        dst2 = lambda hq, ti: out_cT[hq * 4:(hq + 1) * 4, :,
                                     ti * 512:(ti + 1) * 512].rearrange(
                                         "c p s -> p c s")
        dn_era(False, ROW(R_Q2), ROW(R_F2), res_src2, dst2)
        return _finish(p, nc)


def _finish(p, nc):
    snap = dict(p.cnt)
    for eng in ENGINES:
        for s, v in snap.items():
            p.wait(eng, s, v)
    with nc.Block() as block:
        for eng in ENGINES:
            def body(e, eng=eng):
                for f in p.q[eng]:
                    f(e)
            getattr(block, eng)(body)
    return nc


def _host_prep(inputs):
    import ml_dtypes
    d = {}
    A = inputs["L0_ffn_w_gate_up"].reshape(128, 128, 16, 128)
    d["wgu"] = np.ascontiguousarray(A.transpose(0, 3, 2, 1)).reshape(128, 128, 2048)
    A = inputs["L0_ffn_w_down"].reshape(16, 128, 64, 128)
    d["wdn"] = np.ascontiguousarray(A.transpose(2, 3, 0, 1)).reshape(64, 128, 2048)
    for nm, W in (("wk", inputs["L0_attn_Wk"]), ("wv", inputs["L0_attn_Wv"]),
                  ("wr", inputs["L0_attn_Wr"]), ("wo", inputs["L0_attn_Wo"])):
        A = W.reshape(16, 128, 16, 128)
        d[nm] = np.ascontiguousarray(A.transpose(0, 3, 2, 1)).reshape(16, 128, 2048)
    tv = np.stack([np.asarray(inputs[k], np.float32).reshape(16, 128) for k in (
        "L0_attn_time_decay", "L0_attn_time_first", "L0_attn_time_mix_key",
        "L0_attn_time_mix_value", "L0_attn_time_mix_receptance")])
    d["tvecs"] = np.ascontiguousarray(tv)
    d["eyef"] = np.eye(128, dtype=np.float32)
    consts = np.zeros((128, 128), np.float32)
    consts[0, :] = 1.0
    consts[:, 0] = 1.0
    consts[:, 1] = 1e-5
    consts[:, 2] = LN127
    consts[0, 0] = 1.0
    d["consts"] = consts
    d["constsb"] = consts.astype(ml_dtypes.bfloat16)
    d["onesrow"] = np.ones((1, 128), np.float32)
    return d


def kernel(**inputs):
    hidden = np.asarray(inputs["L0_attn_hidden"], np.float32)
    shared = _host_prep({k: np.asarray(v, np.float32) for k, v in inputs.items()
                         if k != "L0_attn_hidden"})
    nc = build()
    in_maps = []
    for b in range(8):
        m = dict(shared)
        m["x_cT"] = np.ascontiguousarray(hidden[b].T)
        in_maps.append(m)
    res = run_bass_kernel_spmd(nc, in_maps, core_ids=list(range(8)))
    out = np.stack([np.ascontiguousarray(
        np.asarray(r["out_cT"]).reshape(2048, 2048).T) for r in res.results])
    return out.astype(np.float32)
